# version 18
# baseline (speedup 1.0000x reference)
import contextlib
import math
import numpy as np
import ml_dtypes
import concourse.bass as bass
import concourse.mybir as mybir
from concourse.bass_utils import run_bass_kernel_spmd

F32 = mybir.dt.float32
BF16 = mybir.dt.bfloat16
AF = mybir.ActivationFunctionType
ALU = mybir.AluOpType
AXX = mybir.AxisListType.X
BFNP = ml_dtypes.bfloat16

B, S, H = 2, 8192, 2048
NH, HD, RD = 16, 128, 64
KVC, QC = 512, 1536
NE, NI = 32, 768
SCALE = float(np.float32(1.0) / np.float32(np.sqrt(np.float32(HD + RD))))
TOK = 2048
NSLOT = 4
SLOT_LEN = (16, 12, 8, 4)
NGRP = 8
GT = 1024
NEG = -1.0e9
HC, LC, QCC, ICC = H // 128, KVC // 128, QC // 128, NI // 128


def _slots_for_core(j):
    return [15 - j, 11 - j, 7 - j, 3 - j]


def build(debug=False, phases=3, ngrp=NGRP, nown=2, ln_only=False, qstage=4):
    nc = bass.Bass("TRN2", target_bir_lowering=False)

    def din(name, shape, dt=F32):
        return nc.dram_tensor(name, list(shape), dt, kind="ExternalInput").ap()

    def dscr(name, shape, dt=BF16):
        kind = "ExternalOutput" if debug else "Internal"
        return nc.dram_tensor(name, list(shape), dt, kind=kind).ap()

    xseq = din("xseq", [S, H])
    xown = din("xown", [TOK, H])
    wkvd = din("wkvd", [H, KVC])
    wku = din("wku", [KVC, H])
    wvu = din("wvu", [KVC, H])
    wkr = din("wkr", [KVC, NH * RD])
    wqd = din("wqd", [H, QC])
    wqu = din("wqu", [QC, H])
    wqr = din("wqr", [QC, NH * RD])
    wo = din("wo", [H, H])
    bkvd = din("bkvd", [128, LC])
    bku = din("bku", [128, HC])
    bkr = din("bkr", [128, 8])
    bqd = din("bqd", [128, QCC])
    bqu = din("bqu", [128, HC])
    bqr = din("bqr", [128, 8])
    bo = din("bo", [128, HC])
    bvu = din("bvu", [1, H])
    gwt = din("gwt", [H, NE])
    wg_t = din("wg_t", [NE, H, NI])
    wu_t = din("wu_t", [NE, H, NI])
    wd_t = din("wd_t", [NE, NI, H])
    c4k = din("c4k", [128, S], BF16)
    s4k = din("s4k", [128, S], BF16)
    c4q = din("c4q", [128, TOK], BF16)
    s4q = din("s4q", [128, TOK], BF16)
    identf = din("identf", [128, 128], F32)
    swp = din("swp", [128, 128], BF16)
    onescol = din("onescol", [128, 1], BF16)
    onesrow = din("onesrow", [1, 128], BF16)
    dneg = din("dneg", [4, 128, 512], BF16)
    flgi = din("flgi", [NSLOT, 4, 128, 128], BF16)
    bflag = din("bflag", [NSLOT, 128, 16], F32)
    oneh = din("oneh", [32, NE * 128], BF16)

    out = nc.dram_tensor("out", [TOK, H], F32, kind="ExternalOutput").ap()

    kc_s = dscr("kc_s", [NH, 128, S])
    kr_s = dscr("kr_s", [8, 128, S])
    v_s = dscr("v_s", [S, H])
    qc_sc = dscr("qc_sc", [NH, 128, TOK])
    qr_sc = dscr("qr_sc", [8, 128, TOK])
    ctx_s = dscr("ctx_s", [NH, 128, TOK])
    dbgp_s = dscr("dbgp_s", [16, 128, 512])
    dbga_s = dscr("dbga_s", [2, 128, HC * GT], F32)
    dbgb_s = dscr("dbgb_s", [2, 128, HC * GT], F32)
    dbgm_s = dscr("dbgm_s", [32, TOK], F32)
    dbgc_s = dscr("dbgc_s", [128, HC * GT], F32)
    dbgd_s = dscr("dbgd_s", [128, 8 * GT])
    xnb_s = dscr("xnb_s", [HC, 128, TOK])
    xnf_s = dscr("xnf_s", [HC, 128, TOK], F32)

    stack = contextlib.ExitStack()
    names = ["bar", "dx", "dq", "dw0", "dw1", "dk0", "dk1", "dst", "dout",
             "tmm", "tsc", "tctx", "aex", "aev", "smi", "vln", "vmi", "gmi"]
    sem = {n: stack.enter_context(nc.semaphore(n)) for n in names}
    cnt = {n: 0 for n in names}

    def inc(instr, n, k=1):
        instr.then_inc(sem[n], k)
        cnt[n] += k
        return cnt[n]

    def wt(eng, n, v):
        if v > 0:
            eng.wait_ge(sem[n], v)

    EN = None

    def barrier():
        cnt["bar"] += 5
        b = cnt["bar"]
        for e in EN:
            e.nop().then_inc(sem["bar"], 1)
        for e in EN:
            e.wait_ge(sem["bar"], b)

    with stack, nc.Block() as blk:
        _e0 = stack.enter_context
        c_identf = _e0(nc.sbuf_tensor("c_identf", [128, 128], F32))
        c_swp = _e0(nc.sbuf_tensor("c_swp", [128, 128], BF16))
        c_ones_c = _e0(nc.sbuf_tensor("c_ones_c", [128, 1], BF16))
        c_ones_r = _e0(nc.sbuf_tensor("c_ones_r", [1, 128], BF16))
        c_bflag = _e0(nc.sbuf_tensor("c_bflag", [128, NSLOT * 16], F32))
        c_bkvd = _e0(nc.sbuf_tensor("c_bkvd", [128, LC], F32))
        c_bku = _e0(nc.sbuf_tensor("c_bku", [128, HC], F32))
        c_bkr = _e0(nc.sbuf_tensor("c_bkr", [128, 8], F32))
        c_bqd = _e0(nc.sbuf_tensor("c_bqd", [128, QCC], F32))
        c_bqu = _e0(nc.sbuf_tensor("c_bqu", [128, HC], F32))
        c_bqr = _e0(nc.sbuf_tensor("c_bqr", [128, 8], F32))
        c_bo = _e0(nc.sbuf_tensor("c_bo", [128, HC], F32))
        c_bvu = _e0(nc.sbuf_tensor("c_bvu", [1, H], BF16))
        c_oneh = _e0(nc.sbuf_tensor("c_oneh", [32, NE * 128], BF16))
        p_M = _e0(nc.sbuf_tensor("p_M", [128, 24], BF16))
        p_mhat = _e0(nc.sbuf_tensor("p_mhat", [128, NSLOT * NH], F32))
        p_mwtb = _e0(nc.sbuf_tensor("p_mwtb", [32, TOK], BF16))
        c_eps = _e0(nc.sbuf_tensor("c_eps", [128, 1], F32))

        EN = [nc.sync, nc.gpsimd, nc.tensor, nc.vector, nc.scalar]
        sy, gp, te, ve, sc = nc.sync, nc.gpsimd, nc.tensor, nc.vector, nc.scalar

        for src, dst in [(identf, c_identf), (swp, c_swp), (onescol, c_ones_c),
                         (onesrow, c_ones_r), (bkvd, c_bkvd), (bku, c_bku),
                         (bkr, c_bkr), (bqd, c_bqd), (bqu, c_bqu), (bqr, c_bqr),
                         (bo, c_bo), (oneh, c_oneh)]:
            inc(sy.dma_start(out=dst[:], in_=src), "dx", 16)
        inc(gp.dma_start(out=c_bvu[:], in_=bvu), "dx", 16)
        inc(sy.dma_start(out=c_bflag[:].rearrange("p (s k) -> p s k", s=NSLOT),
                         in_=bflag.rearrange("s p k -> p s k")), "dx", 16)
        inc(ve.memset(p_M[:], 0.0), "vmi")
        inc(ve.memset(c_eps[:], 1.0e-5), "vmi")
        p0dx, p0vmi = cnt["dx"], cnt["vmi"]
        for e in EN:
            wt(e, "dx", p0dx)
            wt(e, "vmi", p0vmi)

        with contextlib.ExitStack() as p1s:
            _e = p1s.enter_context
            xb = _e(nc.sbuf_tensor("xb", [128, 2 * H], F32))
            xnt = _e(nc.sbuf_tensor("xnt", [128, 2 * H], F32))
            trash = _e(nc.sbuf_tensor("trash", [128, H], F32))
            st1 = _e(nc.sbuf_tensor("st1", [128, 8], F32))
            xnfm = _e(nc.sbuf_tensor("xnfm", [128, HC * GT], BF16))
            kvcb = _e(nc.sbuf_tensor("kvcb", [128, LC * GT], BF16))
            wbufA = _e(nc.sbuf_tensor("wbufA", [128, 8192], BF16))
            wbufB = _e(nc.sbuf_tensor("wbufB", [128, 8192], BF16))
            kst = _e(nc.sbuf_tensor("kst", [128, 2 * GT], BF16))
            krw = _e(nc.sbuf_tensor("krw", [128, GT], BF16))
            krs = _e(nc.sbuf_tensor("krs", [128, GT], BF16))
            km1 = _e(nc.sbuf_tensor("km1", [128, GT], BF16))
            cosb = _e(nc.sbuf_tensor("cosb", [128, GT], BF16))
            sinb = _e(nc.sbuf_tensor("sinb", [128, GT], BF16))
            qcg = _e(nc.sbuf_tensor("qcg", [128, QCC * GT], BF16))
            xf32 = _e(nc.sbuf_tensor("xf32", [128, 2 * H], F32))
            mred = _e(nc.sbuf_tensor("mred", [128, 1], BF16))
            ps_t0 = _e(nc.psum_tensor("ps_t0", [128, 512], F32))
            ps_t1 = _e(nc.psum_tensor("ps_t1", [128, 512], F32))
            ps_a = _e(nc.psum_tensor("ps_a", [128, 1024], F32))
            ps_b = _e(nc.psum_tensor("ps_b", [128, 1024], F32))
            ps_c = _e(nc.psum_tensor("ps_c", [128, 1024], F32))

            ps_tr = [ps_t0, ps_t1]
            ps_mm = [ps_a, ps_b]

            trk = {
                "xn_done": [0, 0],
                "xb_free": [0, 0],
                "tr_ev": [0, 0],
                "tr_ev_v": [0, 0],
                "tr_mm": [0, 0],
                "mm_ev": [0, 0],
                "xf_st": [0, 0],
                "kst_st": [0, 0],
                "wA_mm": 0,
                "wB_mm": 0,
                "kvc_smi": 0,
                "xnfm_aev": 0,
                "krw_v": 0,
                "psc_ev": 0,
                "qcg_smi": 0,
            }

            def ln_tile(g, t, own):
                tb = t % 2
                src = xown if own else xseq
                gof = (g if not own else g) * GT
                wt(sy, "vln", trk["xb_free"][tb])
                dxv = inc(sy.dma_start(out=xb[:, tb * H:(tb + 1) * H],
                                       in_=src[gof + t * 128: gof + (t + 1) * 128, :]), "dx", 16)
                wt(sc, "dx", dxv)
                inc(sc.activation(out=trash[:], in_=xb[:, tb * H:(tb + 1) * H],
                                  func=AF.Identity, accum_out=st1[:, 0:1]), "smi")
                s2 = inc(sc.activation(out=trash[:], in_=xb[:, tb * H:(tb + 1) * H],
                                       func=AF.Square, accum_out=st1[:, 1:2]), "smi")
                wt(ve, "smi", s2)
                i1 = inc(ve.tensor_scalar(out=st1[:, 2:3], in0=st1[:, 0:1], scalar1=1.0 / H,
                                          scalar2=None, op0=ALU.mult), "vmi")
                wt(ve, "vmi", i1)
                i2 = inc(ve.tensor_tensor(out=st1[:, 3:4], in0=st1[:, 2:3], in1=st1[:, 2:3],
                                          op=ALU.mult), "vmi")
                wt(ve, "vmi", i2)
                i3 = inc(ve.tensor_scalar(out=st1[:, 4:5], in0=st1[:, 1:2], scalar1=1.0 / H,
                                          scalar2=st1[:, 3:4], op0=ALU.mult, op1=ALU.subtract), "vmi")
                wt(sc, "vmi", i3)
                s3 = inc(sc.activation(out=st1[:, 5:6], in_=st1[:, 4:5], func=AF.Sqrt,
                                       bias=c_eps[:], scale=1.0), "smi")
                wt(ve, "smi", s3)
                i4 = inc(ve.reciprocal(out=st1[:, 6:7], in_=st1[:, 5:6]), "vmi")
                wt(ve, "vmi", i4)
                wt(ve, "tmm", trk["xn_done"][tb])
                vxn = inc(ve.tensor_scalar(out=xnt[:, tb * H:(tb + 1) * H],
                                           in0=xb[:, tb * H:(tb + 1) * H],
                                           scalar1=st1[:, 2:3], scalar2=st1[:, 6:7],
                                           op0=ALU.subtract, op1=ALU.mult), "vln")
                trk["xb_free"][tb] = vxn

                for q4 in range(4):
                    ip = q4 % 2
                    wt(te, "vln", vxn)
                    wt(te, "aev", trk["tr_ev"][ip])
                    wt(te, "vmi", trk["tr_ev_v"][ip])
                    for ccc in range(4):
                        c = q4 * 4 + ccc
                        ii = te.transpose(out=ps_tr[ip][:, ccc * 128:(ccc + 1) * 128],
                                          in_=xnt[:, tb * H + c * 128: tb * H + (c + 1) * 128],
                                          identity=c_identf[:])
                    ttr = inc(ii, "tmm")
                    trk["tr_mm"][ip] = ttr
                    wt(sc, "tmm", ttr)
                    dst_ap = xnfm[:].rearrange("p (c u) -> p c u", c=HC)[:, q4 * 4:(q4 + 1) * 4,
                                                                        t * 128:(t + 1) * 128]
                    aev = inc(sc.activation(out=dst_ap,
                                            in_=ps_tr[ip][:].rearrange("p (c u) -> p c u", c=4),
                                            func=AF.Copy), "aev")
                    trk["tr_ev"][ip] = aev
                    if own:
                        wt(ve, "tmm", ttr)
                        wt(ve, "aev", aev)
                        wt(ve, "dst", trk["xf_st"][tb])
                        vv = inc(ve.tensor_copy(out=xf32[:, tb * H + q4 * 512: tb * H + (q4 + 1) * 512],
                                                in_=ps_tr[ip][:]), "vmi")
                        trk["tr_ev_v"][ip] = vv
                trk["xn_done"][tb] = trk["tr_mm"][(3) % 2]

                if own:
                    wt(sc, "vmi", trk["tr_ev_v"][1])
                    wt(sc, "vmi", trk["tr_ev_v"][0])
                    srcap = xf32[:, tb * H:(tb + 1) * H].rearrange("p (c u) -> p c u", c=HC)
                    dstap = xnf_s.rearrange("c p u -> p c u")[:, :, g * GT + t * 128: g * GT + (t + 1) * 128]
                    st = inc(sc.dma_start(out=dstap, in_=srcap), "dst", 16)
                    trk["xf_st"][tb] = st

            def wload(buf, src2d, a, pname, waitv):
                wt(gp, "tmm", waitv)
                bwidth = src2d.shape[1]
                return inc(gp.dma_start(out=buf[:, 0:a * bwidth].rearrange("p (a b) -> p a b", a=a),
                                        in_=src2d.rearrange("(a p) b -> p a b", p=128)), pname, 16)

            for g in range(ngrp):
                lwv = wload(wbufA, wkvd, HC, "dw0", trk["wA_mm"])
                for t in range(8):
                    ln_tile(g, t, False)
                xnfm_ready = cnt["aev"]
                if ln_only:
                    continue

                lku = wload(wbufB, wku, LC, "dw1", trk["wB_mm"])
                for l in range(LC):
                    ip = l % 2
                    wt(te, "aev", xnfm_ready)
                    wt(te, "dw0", lwv)
                    wt(te, "smi", trk["mm_ev"][ip])
                    for c in range(HC):
                        for hh in range(2):
                            ii = te.matmul(out=ps_mm[ip][:, hh * 512:(hh + 1) * 512],
                                           lhsT=wbufA[:, c * KVC + l * 128: c * KVC + (l + 1) * 128],
                                           rhs=xnfm[:, c * GT + hh * 512: c * GT + (hh + 1) * 512],
                                           start=(c == 0), stop=(c == HC - 1))
                    tv = inc(ii, "tmm")
                    if l >= LC - 2:
                        trk["wA_mm"] = tv
                    wt(sc, "tmm", tv)
                    ev = inc(sc.activation(out=kvcb[:, l * GT:(l + 1) * GT], in_=ps_mm[ip][:],
                                           func=AF.Identity, bias=c_bkvd[:, l:l + 1]), "smi")
                    trk["mm_ev"][ip] = ev
                kvc_smi = cnt["smi"]

                lkr = wload(wbufA, wkr, LC, "dw0", trk["wA_mm"])
                for h in range(NH):
                    ip = h % 2
                    wt(te, "smi", kvc_smi)
                    wt(te, "dw1", lku)
                    wt(te, "smi", trk["mm_ev"][ip])
                    for l in range(LC):
                        for hh in range(2):
                            ii = te.matmul(out=ps_mm[ip][:, hh * 512:(hh + 1) * 512],
                                           lhsT=wbufB[:, l * H + h * 128: l * H + (h + 1) * 128],
                                           rhs=kvcb[:, l * GT + hh * 512: l * GT + (hh + 1) * 512],
                                           start=(l == 0), stop=(l == LC - 1))
                    tv = inc(ii, "tmm")
                    if h >= NH - 2:
                        trk["wB_mm"] = tv
                    kb2 = h % 2
                    wt(sc, "tmm", tv)
                    wt(sc, "dst", trk["kst_st"][kb2])
                    wt(sc, "vmi", cnt.get("_kstv%d" % kb2, 0))
                    ev = inc(sc.activation(out=kst[:, kb2 * GT:(kb2 + 1) * GT], in_=ps_mm[ip][:],
                                           func=AF.Identity, bias=c_bku[:, h:h + 1]), "smi")
                    trk["mm_ev"][ip] = ev
                    st = inc(sc.dma_start(out=kc_s[h, :, g * GT:(g + 1) * GT],
                                          in_=kst[:, kb2 * GT:(kb2 + 1) * GT]), "dst", 16)
                    trk["kst_st"][kb2] = st
                    wt(ve, "smi", ev)
                    v1 = inc(ve.reduce_max(out=mred[:], in_=kst[:, kb2 * GT:(kb2 + 1) * GT],
                                           axis=AXX, apply_absolute_value=True), "vmi")
                    wt(ve, "vmi", v1)
                    v2 = inc(ve.tensor_tensor(out=p_M[:, h:h + 1], in0=p_M[:, h:h + 1],
                                              in1=mred[:], op=ALU.max), "vmi")
                    cnt["_kstv%d" % kb2] = v2

                lvu = wload(wbufB, wvu, LC, "dw1", trk["wB_mm"])
                wt(sy, "vmi", trk.get("cos_rd", 0))
                inc(sy.dma_start(out=cosb[:], in_=c4k[:, g * GT:(g + 1) * GT]), "dq", 16)
                dqc = inc(sy.dma_start(out=sinb[:], in_=s4k[:, g * GT:(g + 1) * GT]), "dq", 16)
                for p in range(8):
                    ip = p % 2
                    wt(te, "smi", kvc_smi)
                    wt(te, "dw0", lkr)
                    wt(te, "smi", trk["mm_ev"][ip])
                    for l in range(LC):
                        for hh in range(2):
                            ii = te.matmul(out=ps_mm[ip][:, hh * 512:(hh + 1) * 512],
                                           lhsT=wbufA[:, l * (NH * RD) + p * 128: l * (NH * RD) + (p + 1) * 128],
                                           rhs=kvcb[:, l * GT + hh * 512: l * GT + (hh + 1) * 512],
                                           start=(l == 0), stop=(l == LC - 1))
                    tv = inc(ii, "tmm")
                    if p >= 6:
                        trk["wA_mm"] = tv
                    wt(sc, "tmm", tv)
                    wt(sc, "vmi", trk["krw_v"])
                    evr = inc(sc.activation(out=krw[:], in_=ps_mm[ip][:],
                                            func=AF.Identity, bias=c_bkr[:, p:p + 1]), "smi")
                    trk["mm_ev"][ip] = evr
                    wt(te, "smi", evr)
                    wt(te, "smi", trk["psc_ev"])
                    for hh in range(2):
                        ii = te.matmul(out=ps_c[:, hh * 512:(hh + 1) * 512], lhsT=c_swp[:],
                                       rhs=krw[:, hh * 512:(hh + 1) * 512], start=True, stop=True)
                    tsw = inc(ii, "tmm")
                    wt(sc, "tmm", tsw)
                    evs = inc(sc.activation(out=krs[:], in_=ps_c[:], func=AF.Copy), "smi")
                    trk["psc_ev"] = evs
                    kb2 = p % 2
                    wt(ve, "smi", evs)
                    wt(ve, "dq", dqc)
                    v1 = inc(ve.tensor_tensor(out=km1[:], in0=krw[:], in1=cosb[:], op=ALU.mult), "vmi")
                    trk["krw_v"] = v1
                    wt(ve, "vmi", v1)
                    v2 = inc(ve.tensor_tensor(out=krs[:], in0=krs[:], in1=sinb[:], op=ALU.mult), "vmi")
                    trk["cos_rd"] = v2
                    wt(ve, "vmi", v2)
                    wt(ve, "dst", trk["kst_st"][kb2])
                    v3 = inc(ve.tensor_tensor(out=kst[:, kb2 * GT:(kb2 + 1) * GT], in0=km1[:],
                                              in1=krs[:], op=ALU.add), "vmi")
                    wt(ve, "vmi", v3)
                    v4 = inc(ve.reduce_max(out=mred[:], in_=kst[:, kb2 * GT:(kb2 + 1) * GT],
                                           axis=AXX, apply_absolute_value=True), "vmi")
                    wt(ve, "vmi", v4)
                    v5 = inc(ve.tensor_tensor(out=p_M[:, 16 + p:17 + p], in0=p_M[:, 16 + p:17 + p],
                                              in1=mred[:], op=ALU.max), "vmi")
                    cnt["_kstv%d" % kb2] = v5
                    wt(sc, "vmi", v3)
                    st = inc(sc.dma_start(out=kr_s[p, :, g * GT:(g + 1) * GT],
                                          in_=kst[:, kb2 * GT:(kb2 + 1) * GT]), "dst", 16)
                    trk["kst_st"][kb2] = st

                for st8 in range(8):
                    for dh in range(2):
                        ip = (st8 * 2 + dh) % 2
                        wt(te, "dw1", lvu)
                        wt(te, "smi", trk["mm_ev"][ip])
                        for l in range(LC):
                            for hh in range(2):
                                ii = te.matmul(out=ps_mm[ip][:, hh * 512:(hh + 1) * 512],
                                               lhsT=kvcb[:, l * GT + st8 * 128:l * GT + (st8 + 1) * 128],
                                               rhs=wbufB[:, l * H + dh * 1024 + hh * 512:
                                                         l * H + dh * 1024 + (hh + 1) * 512],
                                               start=(l == 0), stop=False)
                        for hh in range(2):
                            ii = te.matmul(out=ps_mm[ip][:, hh * 512:(hh + 1) * 512],
                                           lhsT=c_ones_r[:],
                                           rhs=c_bvu[0:1, dh * 1024 + hh * 512: dh * 1024 + (hh + 1) * 512],
                                           start=False, stop=True)
                        tv = inc(ii, "tmm")
                        kb2 = ip
                        wt(sc, "tmm", tv)
                        wt(sc, "dst", trk["kst_st"][kb2])
                        wt(sc, "vmi", cnt.get("_kstv%d" % kb2, 0))
                        ev = inc(sc.activation(out=kst[:, kb2 * GT:(kb2 + 1) * GT], in_=ps_mm[ip][:],
                                               func=AF.Copy), "smi")
                        trk["mm_ev"][ip] = ev
                        stv = inc(sc.dma_start(out=v_s[g * GT + st8 * 128: g * GT + (st8 + 1) * 128,
                                                       dh * 1024:(dh + 1) * 1024],
                                               in_=kst[:, kb2 * GT:(kb2 + 1) * GT]), "dst", 16)
                        trk["kst_st"][kb2] = stv
                trk["wB_mm"] = cnt["tmm"]

            for og in range(nown):
                for t in range(8):
                    ln_tile(og, t, True)
                xnfm_ready = cnt["aev"]
                if qstage < 1:
                    continue
                wt(sc, "aev", xnfm_ready)
                inc(sc.dma_start(out=xnb_s.rearrange("c p u -> p c u")[:, :, og * GT:(og + 1) * GT],
                                 in_=xnfm[:].rearrange("p (c u) -> p c u", c=HC)), "dst", 16)

                for oc in range(QCC if qstage >= 2 else 0):
                    buf = [wbufA, wbufB][oc % 2]
                    lw = wload(buf, wqd[:, oc * 128:(oc + 1) * 128], HC,
                               "dw0" if oc % 2 == 0 else "dw1",
                               trk["wA_mm"] if oc % 2 == 0 else trk["wB_mm"])
                    ip = oc % 2
                    wt(te, "aev", xnfm_ready)
                    wt(te, "dw0" if oc % 2 == 0 else "dw1", lw)
                    wt(te, "smi", trk["mm_ev"][ip])
                    for c in range(HC):
                        for hh in range(2):
                            ii = te.matmul(out=ps_mm[ip][:, hh * 512:(hh + 1) * 512],
                                           lhsT=buf[:, c * 128:(c + 1) * 128],
                                           rhs=xnfm[:, c * GT + hh * 512: c * GT + (hh + 1) * 512],
                                           start=(c == 0), stop=(c == HC - 1))
                    tv = inc(ii, "tmm")
                    if oc % 2 == 0:
                        trk["wA_mm"] = tv
                    else:
                        trk["wB_mm"] = tv
                    wt(sc, "tmm", tv)
                    wt(sc, "tmm", trk["qcg_smi"])
                    ev = inc(sc.activation(out=qcg[:, oc * GT:(oc + 1) * GT], in_=ps_mm[ip][:],
                                           func=AF.Identity, bias=c_bqd[:, oc:oc + 1]), "smi")
                    trk["mm_ev"][ip] = ev
                qcg_smi = cnt["smi"]

                for h in range(NH if qstage >= 3 else 0):
                    buf = [wbufA, wbufB][h % 2]
                    lw = wload(buf, wqu[:, h * 128:(h + 1) * 128], QCC,
                               "dw0" if h % 2 == 0 else "dw1",
                               trk["wA_mm"] if h % 2 == 0 else trk["wB_mm"])
                    ip = h % 2
                    wt(te, "smi", qcg_smi)
                    wt(te, "dw0" if h % 2 == 0 else "dw1", lw)
                    wt(te, "smi", trk["mm_ev"][ip])
                    for l in range(QCC):
                        for hh in range(2):
                            ii = te.matmul(out=ps_mm[ip][:, hh * 512:(hh + 1) * 512],
                                           lhsT=buf[:, l * 128:(l + 1) * 128],
                                           rhs=qcg[:, l * GT + hh * 512: l * GT + (hh + 1) * 512],
                                           start=(l == 0), stop=(l == QCC - 1))
                    tv = inc(ii, "tmm")
                    if h % 2 == 0:
                        trk["wA_mm"] = tv
                    else:
                        trk["wB_mm"] = tv
                    kb2 = h % 2
                    wt(sc, "tmm", tv)
                    wt(sc, "dst", trk["kst_st"][kb2])
                    wt(sc, "vmi", cnt.get("_kstv%d" % kb2, 0))
                    ev = inc(sc.activation(out=kst[:, kb2 * GT:(kb2 + 1) * GT], in_=ps_mm[ip][:],
                                           func=AF.Identity, bias=c_bqu[:, h:h + 1]), "smi")
                    trk["mm_ev"][ip] = ev
                    stv = inc(sc.dma_start(out=qc_sc[h, :, og * GT:(og + 1) * GT],
                                           in_=kst[:, kb2 * GT:(kb2 + 1) * GT]), "dst", 16)
                    trk["kst_st"][kb2] = stv

                wt(sy, "vmi", trk.get("cos_rd", 0))
                inc(sy.dma_start(out=cosb[:], in_=c4q[:, og * GT:(og + 1) * GT]), "dq", 16)
                dqc = inc(sy.dma_start(out=sinb[:], in_=s4q[:, og * GT:(og + 1) * GT]), "dq", 16)
                for p in range(8 if qstage >= 4 else 0):
                    buf = [wbufA, wbufB][p % 2]
                    lw = wload(buf, wqr[:, p * 128:(p + 1) * 128], QCC,
                               "dw0" if p % 2 == 0 else "dw1",
                               trk["wA_mm"] if p % 2 == 0 else trk["wB_mm"])
                    ip = p % 2
                    wt(te, "smi", qcg_smi)
                    wt(te, "dw0" if p % 2 == 0 else "dw1", lw)
                    wt(te, "smi", trk["mm_ev"][ip])
                    for l in range(QCC):
                        for hh in range(2):
                            ii = te.matmul(out=ps_mm[ip][:, hh * 512:(hh + 1) * 512],
                                           lhsT=buf[:, l * 128:(l + 1) * 128],
                                           rhs=qcg[:, l * GT + hh * 512: l * GT + (hh + 1) * 512],
                                           start=(l == 0), stop=(l == QCC - 1))
                    tv = inc(ii, "tmm")
                    if p % 2 == 0:
                        trk["wA_mm"] = tv
                    else:
                        trk["wB_mm"] = tv
                    wt(sc, "tmm", tv)
                    wt(sc, "vmi", trk["krw_v"])
                    evr = inc(sc.activation(out=krw[:], in_=ps_mm[ip][:],
                                            func=AF.Identity, bias=c_bqr[:, p:p + 1]), "smi")
                    trk["mm_ev"][ip] = evr
                    wt(te, "smi", evr)
                    wt(te, "smi", trk["psc_ev"])
                    for hh in range(2):
                        ii = te.matmul(out=ps_c[:, hh * 512:(hh + 1) * 512], lhsT=c_swp[:],
                                       rhs=krw[:, hh * 512:(hh + 1) * 512], start=True, stop=True)
                    tsw = inc(ii, "tmm")
                    wt(sc, "tmm", tsw)
                    evs = inc(sc.activation(out=krs[:], in_=ps_c[:], func=AF.Copy), "smi")
                    trk["psc_ev"] = evs
                    kb2 = p % 2
                    wt(ve, "smi", evs)
                    wt(ve, "dq", dqc)
                    v1 = inc(ve.tensor_tensor(out=km1[:], in0=krw[:], in1=cosb[:], op=ALU.mult), "vmi")
                    trk["krw_v"] = v1
                    wt(ve, "vmi", v1)
                    v2 = inc(ve.tensor_tensor(out=krs[:], in0=krs[:], in1=sinb[:], op=ALU.mult), "vmi")
                    trk["cos_rd"] = v2
                    wt(ve, "vmi", v2)
                    wt(ve, "dst", trk["kst_st"][kb2])
                    v3 = inc(ve.tensor_tensor(out=kst[:, kb2 * GT:(kb2 + 1) * GT], in0=km1[:],
                                              in1=krs[:], op=ALU.add), "vmi")
                    wt(sc, "vmi", v3)
                    stv = inc(sc.dma_start(out=qr_sc[p, :, og * GT:(og + 1) * GT],
                                           in_=kst[:, kb2 * GT:(kb2 + 1) * GT]), "dst", 16)
                    trk["kst_st"][kb2] = stv
                trk["qcg_smi"] = cnt["tmm"]

            wt(sc, "dst", cnt["dst"])
            barrier()
            p1s.close()

            for p2s in ([contextlib.ExitStack()] if phases >= 2 else []):
                _e = p2s.enter_context
                c_dneg = _e(nc.sbuf_tensor("c_dneg", [128, 4 * 512], BF16))
                c_flgi = _e(nc.sbuf_tensor("c_flgi", [128, 16 * 128], BF16))
                kcb0 = _e(nc.sbuf_tensor("kcb0", [128, 2 * 512], BF16))
                kcb1 = _e(nc.sbuf_tensor("kcb1", [128, 2 * 512], BF16))
                krb = _e(nc.sbuf_tensor("krb", [128, 2 * 512], BF16))
                vb0 = _e(nc.sbuf_tensor("vb0", [128, 2 * 512], BF16))
                vb1 = _e(nc.sbuf_tensor("vb1", [128, 2 * 512], BF16))
                qcal = _e(nc.sbuf_tensor("qcal", [128, NH * TOK], BF16))
                qral = _e(nc.sbuf_tensor("qral", [128, 8 * TOK], BF16))
                aqc = _e(nc.sbuf_tensor("aqc", [128, 2 * 512], BF16))
                aqr = _e(nc.sbuf_tensor("aqr", [128, 512], BF16))
                prb = _e(nc.sbuf_tensor("prb", [128, 2 * 5 * 512], BF16))
                cst = _e(nc.sbuf_tensor("cst", [128, 2 * 512], BF16))
                dnvb = _e(nc.sbuf_tensor("dnvb", [1, 2 * 512], BF16))
                dbc = _e(nc.sbuf_tensor("dbc", [128, 2 * 512], F32))
                bsk = _e(nc.sbuf_tensor("bsk", [128, 2 * 16], F32))
                mx1 = _e(nc.sbuf_tensor("mx1", [1, 2], BF16))
                scp = [[_e(nc.psum_tensor("sc%d_%d" % (hx, j), [128, 512], F32)) for j in range(2)]
                       for hx in range(2)]
                cxp = [_e(nc.psum_tensor("cx%d" % hx, [128, 512], F32)) for hx in range(2)]
                dnp = [_e(nc.psum_tensor("dn%d" % hx, [1, 512], F32)) for hx in range(2)]
                kcb = [kcb0, kcb1]
                vbb = [vb0, vb1]

                inc(sy.dma_start(out=c_dneg[:].rearrange("p (a b) -> p a b", a=4),
                                 in_=dneg.rearrange("a p b -> p a b")), "dq", 16)
                dq_cst = inc(sy.dma_start(out=c_flgi[:].rearrange("p (a b) -> p a b", a=16),
                                          in_=flgi.rearrange("s c p f -> p (s c) f")), "dq", 16)
                wt(te, "dq", dq_cst)
                for h in range(NH):
                    inc(sy.dma_start(out=qcal[:, h * TOK:(h + 1) * TOK], in_=qc_sc[h]), "dq", 16)
                for p in range(8):
                    inc(sy.dma_start(out=qral[:, p * TOK:(p + 1) * TOK], in_=qr_sc[p]), "dq", 16)
                dq_all = cnt["dq"]

                t2 = {
                    "q_rd": 0,
                    "q_ab": 0,
                    "pair_tctx": 0,
                    "pair_aex": 0,
                    "dn_free": [0, 0],
                    "cx_free": [0, 0],
                    "sc0_smi": [0, 0],
                    "dbc_rd": [0, 0],
                    "cst_st": [0, 0],
                    "bsk_rd": 0,
                    "mhat_rd": 0,
                }

                for slot in range(NSLOT):
                    for pr in range(8):
                        h0 = 2 * pr
                        qcb0 = qcal[:, h0 * TOK + slot * 512: h0 * TOK + (slot + 1) * 512]
                        qcb1 = qcal[:, (h0 + 1) * TOK + slot * 512: (h0 + 1) * TOK + (slot + 1) * 512]
                        qrbs = qral[:, pr * TOK + slot * 512: pr * TOK + (slot + 1) * 512]
                        wt(sc, "dq", dq_all)
                        inc(sc.activation(out=aqc[:, 0:512], in_=qcb0, func=AF.Abs), "smi")
                        inc(sc.activation(out=aqc[:, 512:1024], in_=qcb1, func=AF.Abs), "smi")
                        a2 = inc(sc.activation(out=aqr[:], in_=qrbs, func=AF.Abs), "smi")
                        t2["q_ab"] = a2
                        vbk = [0, 0]
                        for hx in range(2):
                            base = hx * 64
                            col = slot * 16 + h0 + hx
                            wt(te, "smi", a2)
                            wt(te, "vmi", t2["dn_free"][hx])
                            te.matmul(out=dnp[hx][:], lhsT=p_M[:, h0 + hx:h0 + hx + 1],
                                      rhs=aqc[:, hx * 512:(hx + 1) * 512], start=True, stop=False)
                            ii = te.matmul(out=dnp[hx][:], lhsT=p_M[base:base + 64, 16 + pr:17 + pr],
                                           rhs=aqr[base:base + 64, :], start=False, stop=True)
                            tpm = inc(ii, "tmm")
                            wt(ve, "tmm", tpm)
                            vmx = inc(ve.reduce_max(out=mx1[0:1, hx:hx + 1], in_=dnp[hx][:], axis=AXX), "vmi")
                            t2["dn_free"][hx] = vmx
                            wt(te, "vmi", vmx)
                            wt(te, "smi", t2["sc0_smi"][hx])
                            wt(te, "aex", t2["pair_aex"])
                            ii = te.matmul(out=scp[hx][0][0:128, 0:1], lhsT=c_ones_r[:],
                                           rhs=mx1[0:1, hx:hx + 1], start=True, stop=True)
                            tpb = inc(ii, "tmm")
                            wt(sc, "tmm", tpb)
                            wt(sc, "vmi", t2["mhat_rd"])
                            evm = inc(sc.activation(out=p_mhat[:, col:col + 1], in_=scp[hx][0][0:128, 0:1],
                                                    func=AF.Identity, scale=-SCALE), "smi")
                            t2["sc0_smi"][hx] = evm
                            wt(ve, "smi", evm)
                            wt(ve, "aex", t2["bsk_rd"])
                            vbk[hx] = inc(ve.tensor_scalar(out=bsk[:, hx * 16:(hx + 1) * 16],
                                                           in0=c_bflag[:, slot * 16:(slot + 1) * 16],
                                                           scalar1=p_mhat[:, col:col + 1], scalar2=None,
                                                           op0=ALU.add), "vmi")
                            t2["mhat_rd"] = vbk[hx]

                        LEN = SLOT_LEN[slot]
                        nsub = LEN * 4
                        kload = [0, 0]
                        aexL = [[], []]
                        tscL = [[], []]
                        ctxL = []
                        dnL = {}

                        def load_kb(kb):
                            kbp = kb % 2
                            wt(sy, "tctx", dnL.get(kb - 2, t2["pair_tctx"]))
                            inc(sy.dma_start(out=kcb[0][:, kbp * 512:(kbp + 1) * 512],
                                             in_=kc_s[h0, :, kb * 512:(kb + 1) * 512]),
                                "dk0" if kbp == 0 else "dk1", 16)
                            inc(sy.dma_start(out=kcb[1][:, kbp * 512:(kbp + 1) * 512],
                                             in_=kc_s[h0 + 1, :, kb * 512:(kb + 1) * 512]),
                                "dk0" if kbp == 0 else "dk1", 16)
                            inc(sy.dma_start(out=krb[:, kbp * 512:(kbp + 1) * 512],
                                             in_=kr_s[pr, :, kb * 512:(kb + 1) * 512]),
                                "dk0" if kbp == 0 else "dk1", 16)
                            for hx in range(2):
                                kload[kbp] = inc(
                                    sy.dma_start(out=vbb[hx][:, kbp * 512:(kbp + 1) * 512].rearrange(
                                                     "p (a b) -> p a b", a=4),
                                                 in_=v_s[kb * 512:(kb + 1) * 512,
                                                         (h0 + hx) * 128:(h0 + hx + 1) * 128].rearrange(
                                                     "(a p) b -> p a b", p=128)),
                                    "dk0" if kbp == 0 else "dk1", 16)

                        load_kb(0)
                        if LEN > 1:
                            load_kb(1)

                        def em_scores(i):
                            kb, ksub = i // 4, i % 4
                            kbp = kb % 2
                            ip = i % 2
                            cand = kb >= LEN - 4
                            wt(te, "dk0" if kbp == 0 else "dk1", kload[kbp])
                            for hx in range(2):
                                if i >= 2:
                                    wt(te, "aex", aexL[hx][i - 2])
                                else:
                                    wt(te, "aex", t2["pair_aex"])
                                    if ip == 0:
                                        wt(te, "smi", t2["sc0_smi"][hx])
                                if i == 0:
                                    wt(te, "vmi", vbk[hx])
                            for hx in range(2):
                                te.matmul(out=scp[hx][ip][:],
                                          lhsT=kcb[hx][:, kbp * 512 + ksub * 128: kbp * 512 + (ksub + 1) * 128],
                                          rhs=[qcb0, qcb1][hx], start=True, stop=False)
                            iis = [None, None]
                            for hx in range(2):
                                base = hx * 64
                                iis[hx] = te.matmul(out=scp[hx][ip][:],
                                                    lhsT=krb[base:base + 64, kbp * 512 + ksub * 128:
                                                             kbp * 512 + (ksub + 1) * 128],
                                                    rhs=qral[base:base + 64,
                                                             pr * TOK + slot * 512: pr * TOK + (slot + 1) * 512],
                                                    start=False, stop=not cand)
                            if cand:
                                ci = kb - (LEN - 4)
                                for hx in range(2):
                                    iis[hx] = te.matmul(out=scp[hx][ip][:],
                                                        lhsT=c_flgi[:, (slot * 4 + ci) * 128:(slot * 4 + ci + 1) * 128],
                                                        rhs=c_dneg[:, ksub * 512:(ksub + 1) * 512],
                                                        start=False, stop=True)
                            for hx in range(2):
                                tscL[hx].append(inc(iis[hx], "tsc"))

                        def em_exp(i):
                            kb = i // 4
                            ip = i % 2
                            s5 = i % 5
                            for hx in range(2):
                                wt(sc, "tsc", tscL[hx][i])
                                if i >= 5:
                                    kbp5 = (i - 5) // 4
                                    wt(sc, "tctx", max(ctxL[i - 5], dnL.get(kbp5, 0)))
                                else:
                                    wt(sc, "tctx", t2["pair_tctx"])
                                if i == 0:
                                    wt(sc, "vmi", vbk[hx])
                                aexL[hx].append(inc(sc.activation(
                                    out=prb[:, (hx * 5 + s5) * 512:(hx * 5 + s5 + 1) * 512],
                                    in_=scp[hx][ip][:], func=AF.Exp,
                                    bias=bsk[:, hx * 16 + kb:hx * 16 + kb + 1], scale=SCALE), "aex"))

                        def em_ctx(i):
                            kb, ksub = i // 4, i % 4
                            kbp = kb % 2
                            s5 = i % 5
                            st_ = (i == 0)
                            sp_ = (i == nsub - 1)
                            for hx in range(2):
                                wt(te, "aex", aexL[hx][i])
                                if i == 0:
                                    wt(te, "vmi", t2["cx_free"][hx])
                                ii = te.matmul(out=cxp[hx][:],
                                               lhsT=vbb[hx][:, kbp * 512 + ksub * 128: kbp * 512 + (ksub + 1) * 128],
                                               rhs=prb[:, (hx * 5 + s5) * 512:(hx * 5 + s5 + 1) * 512],
                                               start=st_, stop=sp_)
                            ctxL.append(inc(ii, "tctx"))

                        def em_denoms(kb):
                            for hx in range(2):
                                for ksub in range(4):
                                    i = kb * 4 + ksub
                                    s5 = i % 5
                                    ii = te.matmul(out=dnp[hx][:], lhsT=c_ones_c[:],
                                                   rhs=prb[:, (hx * 5 + s5) * 512:(hx * 5 + s5 + 1) * 512],
                                                   start=(kb == 0 and ksub == 0),
                                                   stop=(kb == LEN - 1 and ksub == 3))
                            dnL[kb] = inc(ii, "tctx")

                        for i in range(nsub + 1):
                            if i < nsub:
                                em_scores(i)
                                em_exp(i)
                            if i >= 1:
                                em_ctx(i - 1)
                                if (i - 1) % 4 == 3:
                                    kbdone = (i - 1) // 4
                                    em_denoms(kbdone)
                                    if kbdone + 2 < LEN:
                                        load_kb(kbdone + 2)
                        t2["q_rd"] = max(tscL[0][-1], tscL[1][-1])

                        tlast = cnt["tctx"]
                        for hx in range(2):
                            wt(ve, "tctx", tlast)
                            with nc.allow_low_precision(reason="bf16 softmax denominators (common-factor exact)"):
                                vrc = inc(ve.reciprocal(out=dnvb[0:1, hx * 512:(hx + 1) * 512],
                                                        in_=dnp[hx][:]), "vmi")
                            t2["dn_free"][hx] = vrc
                            wt(te, "vmi", vrc)
                            wt(te, "aex", aexL[hx][-1])
                            wt(te, "smi", t2["sc0_smi"][hx])
                            ii = te.matmul(out=scp[hx][0][:], lhsT=c_ones_r[:],
                                           rhs=dnvb[0:1, hx * 512:(hx + 1) * 512], start=True, stop=True)
                            tdb = inc(ii, "tmm")
                            wt(sc, "tmm", tdb)
                            wt(sc, "vmi", t2["dbc_rd"][hx])
                            evd = inc(sc.activation(out=dbc[:, hx * 512:(hx + 1) * 512],
                                                    in_=scp[hx][0][:], func=AF.Copy), "smi")
                            t2["sc0_smi"][hx] = evd
                            wt(ve, "smi", evd)
                            wt(ve, "dst", t2["cst_st"][hx])
                            vct = inc(ve.tensor_tensor(out=cst[:, hx * 512:(hx + 1) * 512],
                                                       in0=cxp[hx][:], in1=dbc[:, hx * 512:(hx + 1) * 512],
                                                       op=ALU.mult), "vmi")
                            t2["cx_free"][hx] = vct
                            t2["dbc_rd"][hx] = vct
                            wt(sc, "vmi", vct)
                            t2["cst_st"][hx] = inc(sc.dma_start(out=ctx_s[h0 + hx, :, slot * 512:(slot + 1) * 512],
                                                                in_=cst[:, hx * 512:(hx + 1) * 512]), "dst", 16)
                        t2["bsk_rd"] = cnt["aex"]
                        t2["pair_aex"] = cnt["aex"]
                        t2["pair_tctx"] = cnt["tctx"]

                wt(sc, "dst", cnt["dst"])
                barrier()
                p2s.close()

            for rts in ([contextlib.ExitStack()] if phases >= 3 else []):
                _e = rts.enter_context
                gw_sb = _e(nc.sbuf_tensor("gw_sb", [128, HC * 32], F32))
                xfb = _e(nc.sbuf_tensor("xfb", [128, 2 * H], F32))
                lgsb = _e(nc.sbuf_tensor("lgsb", [32, TOK], F32))
                ltt = _e(nc.sbuf_tensor("ltt", [128, 32], F32))
                top8 = _e(nc.sbuf_tensor("top8", [128, 8], F32))
                wrk = _e(nc.sbuf_tensor("wrk", [128, 32], F32))
                selb = _e(nc.sbuf_tensor("selb", [128, 32], F32))
                negm = _e(nc.sbuf_tensor("negm", [128, 1], F32))
                pex = _e(nc.sbuf_tensor("pex", [128, 32], F32))
                den8 = _e(nc.sbuf_tensor("den8", [128, 1], F32))
                p_mwt = _e(nc.sbuf_tensor("p_mwt", [32, TOK], F32))
                lg = [_e(nc.psum_tensor("lg%d" % i, [32, 512], F32)) for i in range(4)]
                ptt = _e(nc.psum_tensor("ptt", [128, 32], F32))
                pmw = _e(nc.psum_tensor("pmw", [32, 128], F32))
                dgw = inc(sy.dma_start(out=gw_sb[:].rearrange("p (a b) -> p a b", a=HC),
                                       in_=gwt.rearrange("(a p) b -> p a b", p=128)), "dq", 16)
                xf_rd = [0, 0]
                for c in range(HC):
                    cb = c % 2
                    wt(sy, "tmm", xf_rd[cb])
                    dxl = inc(sy.dma_start(out=xfb[:, cb * H:(cb + 1) * H], in_=xnf_s[c]), "dq", 16)
                    wt(te, "dq", dxl)
                    if c == 0:
                        wt(te, "dq", dgw)
                    for tb4 in range(4):
                        ii = te.matmul(out=lg[tb4][:], lhsT=gw_sb[:, c * 32:(c + 1) * 32],
                                       rhs=xfb[:, cb * H + tb4 * 512: cb * H + (tb4 + 1) * 512],
                                       start=(c == 0), stop=(c == HC - 1))
                    xf_rd[cb] = inc(ii, "tmm")
                tlg = cnt["tmm"]
                wt(sc, "tmm", tlg)
                for tb4 in range(4):
                    ev = inc(sc.activation(out=lgsb[:, tb4 * 512:(tb4 + 1) * 512], in_=lg[tb4][:],
                                           func=AF.Copy), "smi")
                lg_ev = cnt["smi"]

                ptt_rd = 0
                pmw_ev = 0
                for tt in range(16):
                    wt(te, "smi", lg_ev)
                    wt(te, "smi", ptt_rd)
                    ii = te.transpose(out=ptt[:], in_=lgsb[:, tt * 128:(tt + 1) * 128],
                                      identity=c_identf[0:32, 0:32])
                    tv = inc(ii, "tmm")
                    wt(sc, "tmm", tv)
                    ev = inc(sc.activation(out=ltt[:], in_=ptt[:], func=AF.Copy), "smi")
                    ptt_rd = ev
                    wt(ve, "smi", ev)
                    v1 = inc(ve.max(out=top8[:], in_=ltt[:]), "vmi")
                    wt(ve, "vmi", v1)
                    v2 = inc(ve.match_replace(out=wrk[:], in_to_replace=top8[:], in_values=ltt[:],
                                              imm_value=-1.0e30), "vmi")
                    wt(ve, "vmi", v2)
                    v3 = inc(ve.tensor_tensor(out=selb[:], in0=wrk[:], in1=ltt[:], op=ALU.not_equal), "vmi")
                    wt(ve, "vmi", v3)
                    v4 = inc(ve.tensor_scalar(out=negm[:], in0=top8[:, 0:1], scalar1=-1.0,
                                              scalar2=None, op0=ALU.mult), "vmi")
                    wt(sc, "vmi", v4)
                    s1 = inc(sc.activation(out=pex[:], in_=ltt[:], func=AF.Exp,
                                           bias=negm[:], scale=1.0), "smi")
                    wt(ve, "smi", s1)
                    v5 = inc(ve.tensor_tensor(out=pex[:], in0=pex[:], in1=selb[:], op=ALU.mult), "vmi")
                    wt(ve, "vmi", v5)
                    v6 = inc(ve.reduce_sum(out=den8[:], in_=pex[:], axis=AXX), "vmi")
                    wt(ve, "vmi", v6)
                    v7 = inc(ve.reciprocal(out=den8[:], in_=den8[:]), "vmi")
                    wt(ve, "vmi", v7)
                    v8 = inc(ve.tensor_scalar(out=pex[:], in0=pex[:], scalar1=den8[:],
                                              scalar2=None, op0=ALU.mult), "vmi")
                    wt(te, "vmi", v8)
                    wt(te, "smi", pmw_ev)
                    ii = te.transpose(out=pmw[:], in_=pex[:], identity=c_identf[:])
                    tv2 = inc(ii, "tmm")
                    wt(sc, "tmm", tv2)
                    pmw_ev = inc(sc.activation(out=p_mwt[0:32, tt * 128:(tt + 1) * 128],
                                               in_=pmw[:], func=AF.Copy), "smi")
                wt(ve, "smi", pmw_ev)
                inc(ve.tensor_copy(out=p_mwtb[:], in_=p_mwt[:]), "vmi")
                if debug:
                    wt(sc, "smi", pmw_ev)
                    inc(sc.dma_start(out=dbgm_s, in_=p_mwt[:]), "dst", 16)
                barrier()
                rts.close()

            for hf in range(2 if phases >= 3 else 0):
                with contextlib.ExitStack() as hfs:
                    _e = hfs.enter_context
                    acc = _e(nc.sbuf_tensor("acc%d" % hf, [128, HC * GT], F32))
                    wo_scope = contextlib.ExitStack()
                    ctxsb = wo_scope.enter_context(nc.sbuf_tensor("ctxsb%d" % hf, [128, NH * GT], BF16))
                    wob = wo_scope.enter_context(nc.sbuf_tensor("wob%d" % hf, [128, 2 * 2048], BF16))
                    moe_psums = contextlib.ExitStack()
                    pG = [moe_psums.enter_context(nc.psum_tensor("pG%d_%d" % (i, hf), [128, 1024], F32))
                          for i in range(2)]
                    pU = [moe_psums.enter_context(nc.psum_tensor("pU%d_%d" % (i, hf), [128, 1024], F32))
                          for i in range(2)]

                    wo_mm = [0, 0]
                    wo_ev_smi = [0, 0]
                    for dvh in range(NH):
                        inc(sy.dma_start(out=ctxsb[:, dvh * GT:(dvh + 1) * GT],
                                         in_=ctx_s[dvh, :, hf * GT:(hf + 1) * GT]), "dx", 16)
                    dctx = cnt["dx"]
                    for oc in range(HC):
                        ob = oc % 2
                        wt(gp, "tmm", wo_mm[ob])
                        lw = inc(gp.dma_start(out=wob[:, ob * 2048:(ob + 1) * 2048].rearrange(
                                                  "p (a b) -> p a b", a=HC),
                                              in_=wo[:, oc * 128:(oc + 1) * 128].rearrange(
                                                  "(a p) b -> p a b", p=128)),
                                 "dw0", 16)
                        wt(te, "dw0", lw)
                        wt(te, "dx", dctx)
                        wt(te, "smi", wo_ev_smi[ob])
                        for dvh in range(NH):
                            for hh in range(2):
                                ii = te.matmul(out=pG[ob][:, hh * 512:(hh + 1) * 512],
                                               lhsT=wob[:, ob * 2048 + dvh * 128: ob * 2048 + (dvh + 1) * 128],
                                               rhs=ctxsb[:, dvh * GT + hh * 512: dvh * GT + (hh + 1) * 512],
                                               start=(dvh == 0), stop=(dvh == NH - 1))
                        tv = inc(ii, "tmm")
                        wo_mm[ob] = tv
                        wt(sc, "tmm", tv)
                        ev = inc(sc.activation(out=acc[:, oc * GT:(oc + 1) * GT], in_=pG[ob][:],
                                               func=AF.Identity, bias=c_bo[:, oc:oc + 1]), "smi")
                        wo_ev_smi[ob] = ev
                    acc_init = cnt["smi"]
                    acc_init_v = 0
                    if debug:
                        wt(sc, "smi", acc_init)
                        wt(sc, "vmi", acc_init_v)
                        inc(sc.dma_start(out=dbga_s[hf], in_=acc[:]), "dst", 16)
                    wo_scope.close()
                    moe_sb = contextlib.ExitStack()
                    _em = moe_sb.enter_context
                    xnm = _em(nc.sbuf_tensor("xnm%d" % hf, [128, HC * GT], BF16))
                    wgb = _em(nc.sbuf_tensor("wgb%d" % hf, [128, HC * NI], BF16))
                    wub = _em(nc.sbuf_tensor("wub%d" % hf, [128, HC * NI], BF16))
                    wdb = _em(nc.sbuf_tensor("wdb%d" % hf, [128, ICC * H], BF16))
                    mwsb = _em(nc.sbuf_tensor("mwsb%d" % hf, [128, GT], F32))
                    umb = _em(nc.sbuf_tensor("umb%d" % hf, [128, GT], BF16))
                    actb = _em(nc.sbuf_tensor("actb%d" % hf, [128, ICC * GT], BF16))
                    wt(sy, "tmm", max(wo_mm))
                    wt(sy, "smi", max(wo_ev_smi))
                    for c in range(HC):
                        inc(sy.dma_start(out=xnm[:, c * GT:(c + 1) * GT],
                                         in_=xnb_s[c, :, hf * GT:(hf + 1) * GT]), "dx", 16)
                    dld = cnt["dx"]

                    m3 = {
                        "wg_mm": 0, "wu_mm": 0, "wd_mm": 0,
                        "g_ev": [0, 0], "u_ev": [0, 0],
                        "g_ev_v": [0, 0],
                        "d_ev": [0, 0],
                        "mw_ev": 0,
                        "sil_rd": 0, "um_rd": 0,
                        "act_rd": 0,
                        "acc_v": [0] * HC,
                    }
                    lgw = luw = ldw = 0
                    wt(gp, "tmm", max(wo_mm))
                    wt(gp, "smi", max(wo_ev_smi))
                    for e in range(NE):
                        wt(gp, "tmm", m3["wg_mm"])
                        lgw = inc(gp.dma_start(out=wgb[:].rearrange("p (a b) -> p a b", a=HC),
                                               in_=wg_t[e].rearrange("(a p) b -> p a b", p=128)), "dw0", 16)
                        wt(gp, "tmm", m3["wu_mm"])
                        luw = inc(gp.dma_start(out=wub[:].rearrange("p (a b) -> p a b", a=HC),
                                               in_=wu_t[e].rearrange("(a p) b -> p a b", p=128)), "dw0", 16)
                        wt(gp, "tctx", m3["wd_mm"])
                        ldw = inc(gp.dma_start(out=wdb[:].rearrange("p (a b) -> p a b", a=ICC),
                                               in_=wd_t[e].rearrange("(a p) b -> p a b", p=128)), "dw1", 16)

                        wt(te, "smi", m3["g_ev"][0])
                        wt(te, "vmi", m3["g_ev_v"][0])
                        m3["g_ev_v"][0] = 0
                        for hh in range(2):
                            ii = te.matmul(out=pG[0][:, hh * 512:(hh + 1) * 512],
                                           lhsT=c_oneh[0:32, e * 128:(e + 1) * 128],
                                           rhs=p_mwtb[0:32, hf * GT + hh * 512: hf * GT + (hh + 1) * 512],
                                           start=True, stop=True)
                        tmw = inc(ii, "tmm")
                        wt(sc, "tmm", tmw)
                        wt(sc, "vmi", m3["um_rd"])
                        mwe = inc(sc.activation(out=mwsb[:], in_=pG[0][:], func=AF.Copy), "smi")
                        m3["g_ev"][0] = mwe
                        m3["mw_ev"] = mwe

                        for ic in range(ICC):
                            ib = ic % 2
                            wt(te, "dw0", luw)
                            wt(te, "dx", dld)
                            wt(te, "smi", m3["g_ev"][ib])
                            wt(te, "vmi", m3["g_ev_v"][ib])
                            m3["g_ev_v"][ib] = 0
                            for c in range(HC):
                                for hh in range(2):
                                    ii = te.matmul(out=pG[ib][:, hh * 512:(hh + 1) * 512],
                                                   lhsT=wgb[:, c * NI + ic * 128: c * NI + (ic + 1) * 128],
                                                   rhs=xnm[:, c * GT + hh * 512: c * GT + (hh + 1) * 512],
                                                   start=(c == 0), stop=(c == HC - 1))
                            tg = inc(ii, "tmm")
                            if ic >= ICC - 2:
                                m3["wg_mm"] = tg
                            wt(te, "vmi", m3["u_ev"][ib])
                            for c in range(HC):
                                for hh in range(2):
                                    ii = te.matmul(out=pU[ib][:, hh * 512:(hh + 1) * 512],
                                                   lhsT=wub[:, c * NI + ic * 128: c * NI + (ic + 1) * 128],
                                                   rhs=xnm[:, c * GT + hh * 512: c * GT + (hh + 1) * 512],
                                                   start=(c == 0), stop=(c == HC - 1))
                            tu = inc(ii, "tmm")
                            if ic >= ICC - 2:
                                m3["wu_mm"] = tu
                            wt(sc, "tmm", tg)
                            wt(sc, "tctx", m3["act_rd"])
                            wt(sc, "vmi", m3["sil_rd"])
                            sv = inc(sc.activation(out=actb[:, ic * GT:(ic + 1) * GT],
                                                   in_=pG[ib][:], func=AF.Silu), "smi")
                            m3["g_ev"][ib] = sv
                            wt(ve, "tmm", tu)
                            wt(ve, "smi", mwe)
                            wt(ve, "vmi", m3["um_rd"])
                            uv = inc(ve.tensor_tensor(out=umb[:], in0=pU[ib][:], in1=mwsb[:],
                                                      op=ALU.mult), "vmi")
                            m3["u_ev"][ib] = uv
                            wt(ve, "smi", sv)
                            wt(ve, "vmi", uv)
                            av = inc(ve.tensor_tensor(out=actb[:, ic * GT:(ic + 1) * GT],
                                                      in0=actb[:, ic * GT:(ic + 1) * GT],
                                                      in1=umb[:], op=ALU.mult), "vmi")
                            m3["sil_rd"] = av
                            m3["um_rd"] = av
                        act_v = cnt["vmi"]

                        for hc in range(HC):
                            db = hc % 2
                            wt(te, "vmi", act_v)
                            wt(te, "dw1", ldw)
                            wt(te, "vmi", m3["d_ev"][db])
                            wt(te, "vmi", m3["u_ev"][db])
                            for ic in range(ICC):
                                for hh in range(2):
                                    ii = te.matmul(out=pU[db][:, hh * 512:(hh + 1) * 512],
                                                   lhsT=wdb[:, ic * H + hc * 128: ic * H + (hc + 1) * 128],
                                                   rhs=actb[:, ic * GT + hh * 512: ic * GT + (hh + 1) * 512],
                                                   start=(ic == 0), stop=(ic == ICC - 1))
                            td = inc(ii, "tctx")
                            if hc == HC - 1:
                                m3["wd_mm"] = td
                            wt(ve, "tctx", td)
                            if e == 0:
                                wt(ve, "smi", acc_init)
                                wt(ve, "vmi", acc_init_v)
                            wt(ve, "vmi", m3["acc_v"][hc])
                            avv = inc(ve.tensor_tensor(out=acc[:, hc * GT:(hc + 1) * GT],
                                                       in0=pU[db][:], in1=acc[:, hc * GT:(hc + 1) * GT],
                                                       op=ALU.add), "vmi")
                            m3["acc_v"][hc] = avv
                            m3["d_ev"][db] = avv
                            m3["u_ev"][db] = avv
                        m3["act_rd"] = cnt["tctx"]
                    moe_done = cnt["vmi"]
                    if debug:
                        wt(sc, "vmi", moe_done)
                        inc(sc.dma_start(out=dbgb_s[hf], in_=acc[:]), "dst", 16)
                    moe_psums.close()
                    moe_sb.close()

                    with contextlib.ExitStack() as fns:
                        _e = fns.enter_context
                        xtm = _e(nc.sbuf_tensor("xtm%d" % hf, [128, 2 * H], F32))
                        osb = _e(nc.sbuf_tensor("osb%d" % hf, [128, 2 * H], F32))
                        ps_f = [_e(nc.psum_tensor("pf%d_%d" % (i, hf), [128, 512], F32)) for i in range(2)]
                        fin = {"tr_mm": [0, 0], "tr_rd": [0, 0], "o_st": [0, 0], "x_rd": [0, 0]}
                        wt(sy, "vmi", moe_done)
                        for st8 in range(8):
                            tb = st8 % 2
                            wt(sy, "vmi", fin["x_rd"][tb])
                            dxo = inc(sy.dma_start(out=xtm[:, tb * H:(tb + 1) * H],
                                                   in_=xown[hf * GT + st8 * 128: hf * GT + (st8 + 1) * 128, :]),
                                      "dq", 16)
                            for q4 in range(4):
                                ipt = q4 % 2
                                wt(te, "vmi", moe_done)
                                wt(te, "vmi", fin["tr_rd"][ipt])
                                for ccc in range(4):
                                    c = q4 * 4 + ccc
                                    ii = te.transpose(out=ps_f[ipt][:, ccc * 128:(ccc + 1) * 128],
                                                      in_=acc[:, c * GT + st8 * 128: c * GT + (st8 + 1) * 128],
                                                      identity=c_identf[:])
                                tv = inc(ii, "tmm")
                                wt(ve, "tmm", tv)
                                wt(ve, "dq", dxo)
                                wt(ve, "dout", fin["o_st"][tb])
                                vv = inc(ve.tensor_tensor(out=osb[:, tb * H + q4 * 512: tb * H + (q4 + 1) * 512],
                                                          in0=ps_f[ipt][:],
                                                          in1=xtm[:, tb * H + q4 * 512: tb * H + (q4 + 1) * 512],
                                                          op=ALU.add), "vmi")
                                fin["tr_rd"][ipt] = vv
                            fin["x_rd"][tb] = cnt["vmi"]
                            wt(sc, "vmi", cnt["vmi"])
                            fin["o_st"][tb] = inc(sc.dma_start(out=out[hf * GT + st8 * 128: hf * GT + (st8 + 1) * 128, :],
                                                               in_=osb[:, tb * H:(tb + 1) * H]), "dout", 16)
                    barrier()

            wt(sc, "dout", cnt["dout"])

    return nc


def _host_inputs(inputs):
    x = np.asarray(inputs["L0_attn_x"], dtype=np.float32)

    def f32(a):
        return np.ascontiguousarray(np.asarray(a, dtype=np.float32))

    def bias_pt(b, nchunk):
        return np.ascontiguousarray(np.asarray(b, np.float32).reshape(nchunk, 128).T)

    shared = {
        "wkvd": f32(inputs["L0_attn_W_kvd"]),
        "wku": f32(inputs["L0_attn_W_ku"]),
        "wvu": f32(inputs["L0_attn_W_vu"]),
        "wkr": f32(inputs["L0_attn_W_kr"]),
        "wqd": f32(inputs["L0_attn_W_qd"]),
        "wqu": f32(inputs["L0_attn_W_qu"]),
        "wqr": f32(inputs["L0_attn_W_qr"]),
        "wo": f32(inputs["L0_attn_W_o"]),
        "bkvd": bias_pt(inputs["L0_attn_b_kvd"], LC),
        "bku": bias_pt(inputs["L0_attn_b_ku"], HC),
        "bkr": bias_pt(inputs["L0_attn_b_kr"], 8),
        "bqd": bias_pt(inputs["L0_attn_b_qd"], QCC),
        "bqu": bias_pt(inputs["L0_attn_b_qu"], HC),
        "bqr": bias_pt(inputs["L0_attn_b_qr"], 8),
        "bo": bias_pt(inputs["L0_attn_b_o"], HC),
        "bvu": f32(inputs["L0_attn_b_vu"]).reshape(1, H),
        "gwt": np.ascontiguousarray(f32(inputs["L0_ffn_gate_w"]).T),
        "wg_t": np.ascontiguousarray(f32(inputs["L0_ffn_w_gate_proj"]).transpose(0, 2, 1)),
        "wu_t": np.ascontiguousarray(f32(inputs["L0_ffn_w_up_proj"]).transpose(0, 2, 1)),
        "wd_t": np.ascontiguousarray(f32(inputs["L0_ffn_w_down_proj"]).transpose(0, 2, 1)),
    }

    inv = (10000.0 ** (-np.arange(0, RD, 2, dtype=np.float32) / RD)).astype(np.float32)
    pos = np.arange(S, dtype=np.float32)
    ang = pos[:, None] * inv[None, :]
    ck = np.cos(ang).T.astype(np.float32)
    sk = np.sin(ang).T.astype(np.float32)
    c4k = np.concatenate([ck, ck, ck, ck], axis=0).astype(BFNP)
    s4k = np.concatenate([-sk, sk, -sk, sk], axis=0).astype(BFNP)

    shared.update({
        "c4k": c4k, "s4k": s4k,
        "identf": np.eye(128, dtype=np.float32),
        "swp": np.eye(128, dtype=np.float32)[np.arange(128) ^ 32].T.astype(BFNP),
        "onescol": np.ones((128, 1), BFNP),
        "onesrow": np.ones((1, 128), BFNP),
    })
    swpm = np.zeros((128, 128), np.float32)
    swpm[np.arange(128) ^ 32, np.arange(128)] = 1.0
    shared["swp"] = swpm.astype(BFNP)

    dnegm = np.zeros((4, 128, 512), np.float32)
    for j in range(4):
        kpos = 128 * j + np.arange(128)
        qpos = np.arange(512)
        dnegm[j] = np.where(kpos[:, None] > qpos[None, :], np.float32(NEG), 0.0)
    shared["dneg"] = dnegm.astype(BFNP)

    onehm = np.zeros((32, NE * 128), np.float32)
    for e in range(NE):
        onehm[e, e * 128:(e + 1) * 128] = 1.0
    shared["oneh"] = onehm.astype(BFNP)

    in_maps = []
    metas = []
    for core in range(NCORE):
        b, j = core // 4, core % 4
        slots = _slots_for_core(j)
        own_pos = np.concatenate([np.arange(512 * s, 512 * (s + 1)) for s in slots])
        d = dict(shared)
        d["xseq"] = np.ascontiguousarray(x[b])
        d["xown"] = np.ascontiguousarray(x[b][own_pos])
        d["c4q"] = np.ascontiguousarray(np.concatenate([ck, ck, ck, ck], 0)[:, own_pos]).astype(BFNP)
        sq = np.concatenate([-sk, sk, -sk, sk], 0)
        d["s4q"] = np.ascontiguousarray(sq[:, own_pos]).astype(BFNP)
        flgim = np.zeros((NSLOT, 4, 128, 128), np.float32)
        bflagm = np.zeros((NSLOT, 128, 16), np.float32)
        for sl in range(NSLOT):
            count = slots[sl] + 1
            for ci in range(4):
                kb = SLOT_LEN[sl] - 4 + ci
                if kb == count - 1:
                    flgim[sl, ci] = np.eye(128, dtype=np.float32)
            for kb in range(16):
                if kb >= count:
                    bflagm[sl, :, kb] = NEG
        d["flgi"] = flgim.astype(BFNP)
        d["bflag"] = bflagm
        in_maps.append(d)
        metas.append((b, own_pos))
    return in_maps, metas


NCORE = 8


def kernel(**inputs):
    in_maps, metas = _host_inputs(inputs)
    nc = build(debug=False)
    res = run_bass_kernel_spmd(nc, in_maps, core_ids=list(range(NCORE)))
    outp = np.zeros((B, S, H), dtype=np.float32)
    for core in range(NCORE):
        b, own_pos = metas[core]
        outp[b, own_pos, :] = res.results[core]["out"]
    return outp


# version 19
# speedup vs baseline: 1.0824x; 1.0824x over previous
import contextlib
import math
import numpy as np
import ml_dtypes
import concourse.bass as bass
import concourse.mybir as mybir
from concourse.bass_utils import run_bass_kernel_spmd

F32 = mybir.dt.float32
BF16 = mybir.dt.bfloat16
AF = mybir.ActivationFunctionType
ALU = mybir.AluOpType
AXX = mybir.AxisListType.X
BFNP = ml_dtypes.bfloat16

B, S, H = 2, 8192, 2048
NH, HD, RD = 16, 128, 64
KVC, QC = 512, 1536
NE, NI = 32, 768
SCALE = float(np.float32(1.0) / np.float32(np.sqrt(np.float32(HD + RD))))
TOK = 2048
NSLOT = 4
SLOT_LEN = (16, 12, 8, 4)
NGRP = 8
GT = 1024
NEG = -1.0e9
HC, LC, QCC, ICC = H // 128, KVC // 128, QC // 128, NI // 128


def _slots_for_core(j):
    return [15 - j, 11 - j, 7 - j, 3 - j]


def build(debug=False, phases=3, ngrp=NGRP, nown=2, ln_only=False, qstage=4):
    nc = bass.Bass("TRN2", target_bir_lowering=False)

    def din(name, shape, dt=F32):
        return nc.dram_tensor(name, list(shape), dt, kind="ExternalInput").ap()

    def dscr(name, shape, dt=BF16):
        kind = "ExternalOutput" if debug else "Internal"
        return nc.dram_tensor(name, list(shape), dt, kind=kind).ap()

    xseq = din("xseq", [S, H])
    xown = din("xown", [TOK, H])
    wkvd = din("wkvd", [H, KVC])
    wku = din("wku", [KVC, H])
    wvu = din("wvu", [KVC, H])
    wkr = din("wkr", [KVC, NH * RD])
    wqd = din("wqd", [H, QC])
    wqu = din("wqu", [QC, H])
    wqr = din("wqr", [QC, NH * RD])
    wo = din("wo", [H, H])
    bkvd = din("bkvd", [128, LC])
    bku = din("bku", [128, HC])
    bkr = din("bkr", [128, 8])
    bqd = din("bqd", [128, QCC])
    bqu = din("bqu", [128, HC])
    bqr = din("bqr", [128, 8])
    bo = din("bo", [128, HC])
    bvu = din("bvu", [1, H])
    gwt = din("gwt", [H, NE])
    wg_t = din("wg_t", [NE, H, NI])
    wu_t = din("wu_t", [NE, H, NI])
    wd_t = din("wd_t", [NE, NI, H])
    c4k = din("c4k", [128, S], BF16)
    s4k = din("s4k", [128, S], BF16)
    c4q = din("c4q", [128, TOK], BF16)
    s4q = din("s4q", [128, TOK], BF16)
    identf = din("identf", [128, 128], F32)
    swp = din("swp", [128, 128], BF16)
    onescol = din("onescol", [128, 1], BF16)
    onesrow = din("onesrow", [1, 128], BF16)
    dneg = din("dneg", [4, 128, 512], BF16)
    flgi = din("flgi", [NSLOT, 4, 128, 128], BF16)
    bflag = din("bflag", [NSLOT, 128, 16], F32)
    oneh = din("oneh", [32, NE * 128], BF16)

    out = nc.dram_tensor("out", [TOK, H], F32, kind="ExternalOutput").ap()

    kc_s = dscr("kc_s", [NH, 128, S])
    kr_s = dscr("kr_s", [8, 128, S])
    v_s = dscr("v_s", [S, H])
    qc_sc = dscr("qc_sc", [NH, 128, TOK])
    qr_sc = dscr("qr_sc", [8, 128, TOK])
    ctx_s = dscr("ctx_s", [NH, 128, TOK])
    dbgp_s = dscr("dbgp_s", [16, 128, 512])
    dbga_s = dscr("dbga_s", [2, 128, HC * GT], F32)
    dbgb_s = dscr("dbgb_s", [2, 128, HC * GT], F32)
    dbgm_s = dscr("dbgm_s", [32, TOK], F32)
    dbgc_s = dscr("dbgc_s", [128, HC * GT], F32)
    dbgd_s = dscr("dbgd_s", [128, 8 * GT])
    xnb_s = dscr("xnb_s", [HC, 128, TOK])
    xnf_s = dscr("xnf_s", [HC, 128, TOK], F32)

    stack = contextlib.ExitStack()
    names = ["bar", "dx", "dq", "dw0", "dw1", "dk0", "dk1", "dst", "dout",
             "tmm", "tsc", "tctx", "aex", "aev", "smi", "vln", "vmi", "gmi"]
    sem = {n: stack.enter_context(nc.semaphore(n)) for n in names}
    cnt = {n: 0 for n in names}

    def inc(instr, n, k=1):
        instr.then_inc(sem[n], k)
        cnt[n] += k
        return cnt[n]

    def wt(eng, n, v):
        if v > 0:
            eng.wait_ge(sem[n], v)

    EN = None

    def barrier():
        cnt["bar"] += 5
        b = cnt["bar"]
        for e in EN:
            e.nop().then_inc(sem["bar"], 1)
        for e in EN:
            e.wait_ge(sem["bar"], b)

    with stack, nc.Block() as blk:
        _e0 = stack.enter_context
        c_identf = _e0(nc.sbuf_tensor("c_identf", [128, 128], F32))
        c_swp = _e0(nc.sbuf_tensor("c_swp", [128, 128], BF16))
        c_ones_c = _e0(nc.sbuf_tensor("c_ones_c", [128, 1], BF16))
        c_ones_r = _e0(nc.sbuf_tensor("c_ones_r", [1, 128], BF16))
        c_bflag = _e0(nc.sbuf_tensor("c_bflag", [128, NSLOT * 16], F32))
        c_bkvd = _e0(nc.sbuf_tensor("c_bkvd", [128, LC], F32))
        c_bku = _e0(nc.sbuf_tensor("c_bku", [128, HC], F32))
        c_bkr = _e0(nc.sbuf_tensor("c_bkr", [128, 8], F32))
        c_bqd = _e0(nc.sbuf_tensor("c_bqd", [128, QCC], F32))
        c_bqu = _e0(nc.sbuf_tensor("c_bqu", [128, HC], F32))
        c_bqr = _e0(nc.sbuf_tensor("c_bqr", [128, 8], F32))
        c_bo = _e0(nc.sbuf_tensor("c_bo", [128, HC], F32))
        c_bvu = _e0(nc.sbuf_tensor("c_bvu", [1, H], BF16))
        c_oneh = _e0(nc.sbuf_tensor("c_oneh", [32, NE * 128], BF16))
        p_M = _e0(nc.sbuf_tensor("p_M", [128, 24], BF16))
        p_mhat = _e0(nc.sbuf_tensor("p_mhat", [128, NSLOT * NH], F32))
        p_mwtb = _e0(nc.sbuf_tensor("p_mwtb", [32, TOK], BF16))
        c_eps = _e0(nc.sbuf_tensor("c_eps", [128, 1], F32))

        EN = [nc.sync, nc.gpsimd, nc.tensor, nc.vector, nc.scalar]
        sy, gp, te, ve, sc = nc.sync, nc.gpsimd, nc.tensor, nc.vector, nc.scalar

        for src, dst in [(identf, c_identf), (swp, c_swp), (onescol, c_ones_c),
                         (onesrow, c_ones_r), (bkvd, c_bkvd), (bku, c_bku),
                         (bkr, c_bkr), (bqd, c_bqd), (bqu, c_bqu), (bqr, c_bqr),
                         (bo, c_bo), (oneh, c_oneh)]:
            inc(sy.dma_start(out=dst[:], in_=src), "dx", 16)
        inc(gp.dma_start(out=c_bvu[:], in_=bvu), "dx", 16)
        inc(sy.dma_start(out=c_bflag[:].rearrange("p (s k) -> p s k", s=NSLOT),
                         in_=bflag.rearrange("s p k -> p s k")), "dx", 16)
        inc(ve.memset(p_M[:], 0.0), "vmi")
        inc(ve.memset(c_eps[:], 1.0e-5), "vmi")
        p0dx, p0vmi = cnt["dx"], cnt["vmi"]
        for e in EN:
            wt(e, "dx", p0dx)
            wt(e, "vmi", p0vmi)

        with contextlib.ExitStack() as p1s:
            _e = p1s.enter_context
            xb = _e(nc.sbuf_tensor("xb", [128, 2 * H], F32))
            xnt = _e(nc.sbuf_tensor("xnt", [128, 2 * H], F32))
            trash = _e(nc.sbuf_tensor("trash", [128, H], F32))
            st1 = _e(nc.sbuf_tensor("st1", [128, 8], F32))
            xnfm = _e(nc.sbuf_tensor("xnfm", [128, HC * GT], BF16))
            kvcb = _e(nc.sbuf_tensor("kvcb", [128, LC * GT], BF16))
            wbufA = _e(nc.sbuf_tensor("wbufA", [128, 8192], BF16))
            wbufB = _e(nc.sbuf_tensor("wbufB", [128, 8192], BF16))
            kst = _e(nc.sbuf_tensor("kst", [128, 2 * GT], BF16))
            krw = _e(nc.sbuf_tensor("krw", [128, GT], BF16))
            krs = _e(nc.sbuf_tensor("krs", [128, GT], BF16))
            km1 = _e(nc.sbuf_tensor("km1", [128, GT], BF16))
            cosb = _e(nc.sbuf_tensor("cosb", [128, GT], BF16))
            sinb = _e(nc.sbuf_tensor("sinb", [128, GT], BF16))
            qcg = _e(nc.sbuf_tensor("qcg", [128, QCC * GT], BF16))
            xf32 = _e(nc.sbuf_tensor("xf32", [128, 2 * H], F32))
            mred = _e(nc.sbuf_tensor("mred", [128, 1], BF16))
            ps_t0 = _e(nc.psum_tensor("ps_t0", [128, 512], F32))
            ps_t1 = _e(nc.psum_tensor("ps_t1", [128, 512], F32))
            ps_a = _e(nc.psum_tensor("ps_a", [128, 1024], F32))
            ps_b = _e(nc.psum_tensor("ps_b", [128, 1024], F32))
            ps_c = _e(nc.psum_tensor("ps_c", [128, 1024], F32))

            ps_tr = [ps_t0, ps_t1]
            ps_mm = [ps_a, ps_b]

            trk = {
                "xn_done": [0, 0],
                "xb_free": [0, 0],
                "tr_ev": [0, 0],
                "tr_ev_v": [0, 0],
                "tr_mm": [0, 0],
                "mm_ev": [0, 0],
                "xf_st": [0, 0],
                "kst_st": [0, 0],
                "wA_mm": 0,
                "wB_mm": 0,
                "kvc_smi": 0,
                "xnfm_aev": 0,
                "krw_v": 0,
                "psc_ev": 0,
                "qcg_smi": 0,
            }

            def ln_tile(g, t, own):
                tb = t % 2
                src = xown if own else xseq
                gof = (g if not own else g) * GT
                wt(sy, "vln", trk["xb_free"][tb])
                dxv = inc(sy.dma_start(out=xb[:, tb * H:(tb + 1) * H],
                                       in_=src[gof + t * 128: gof + (t + 1) * 128, :]), "dx", 16)
                wt(sc, "dx", dxv)
                inc(sc.activation(out=trash[:], in_=xb[:, tb * H:(tb + 1) * H],
                                  func=AF.Identity, accum_out=st1[:, 0:1]), "smi")
                s2 = inc(sc.activation(out=trash[:], in_=xb[:, tb * H:(tb + 1) * H],
                                       func=AF.Square, accum_out=st1[:, 1:2]), "smi")
                wt(ve, "smi", s2)
                i1 = inc(ve.tensor_scalar(out=st1[:, 2:3], in0=st1[:, 0:1], scalar1=1.0 / H,
                                          scalar2=None, op0=ALU.mult), "vmi")
                wt(ve, "vmi", i1)
                i2 = inc(ve.tensor_tensor(out=st1[:, 3:4], in0=st1[:, 2:3], in1=st1[:, 2:3],
                                          op=ALU.mult), "vmi")
                wt(ve, "vmi", i2)
                i3 = inc(ve.tensor_scalar(out=st1[:, 4:5], in0=st1[:, 1:2], scalar1=1.0 / H,
                                          scalar2=st1[:, 3:4], op0=ALU.mult, op1=ALU.subtract), "vmi")
                wt(sc, "vmi", i3)
                s3 = inc(sc.activation(out=st1[:, 5:6], in_=st1[:, 4:5], func=AF.Sqrt,
                                       bias=c_eps[:], scale=1.0), "smi")
                wt(ve, "smi", s3)
                i4 = inc(ve.reciprocal(out=st1[:, 6:7], in_=st1[:, 5:6]), "vmi")
                wt(ve, "vmi", i4)
                wt(ve, "tmm", trk["xn_done"][tb])
                vxn = inc(ve.tensor_scalar(out=xnt[:, tb * H:(tb + 1) * H],
                                           in0=xb[:, tb * H:(tb + 1) * H],
                                           scalar1=st1[:, 2:3], scalar2=st1[:, 6:7],
                                           op0=ALU.subtract, op1=ALU.mult), "vln")
                trk["xb_free"][tb] = vxn

                for q4 in range(4):
                    ip = q4 % 2
                    wt(te, "vln", vxn)
                    wt(te, "aev", trk["tr_ev"][ip])
                    wt(te, "vmi", trk["tr_ev_v"][ip])
                    for ccc in range(4):
                        c = q4 * 4 + ccc
                        ii = te.transpose(out=ps_tr[ip][:, ccc * 128:(ccc + 1) * 128],
                                          in_=xnt[:, tb * H + c * 128: tb * H + (c + 1) * 128],
                                          identity=c_identf[:])
                    ttr = inc(ii, "tmm")
                    trk["tr_mm"][ip] = ttr
                    wt(sc, "tmm", ttr)
                    dst_ap = xnfm[:].rearrange("p (c u) -> p c u", c=HC)[:, q4 * 4:(q4 + 1) * 4,
                                                                        t * 128:(t + 1) * 128]
                    aev = inc(sc.activation(out=dst_ap,
                                            in_=ps_tr[ip][:].rearrange("p (c u) -> p c u", c=4),
                                            func=AF.Copy), "aev")
                    trk["tr_ev"][ip] = aev
                    if own:
                        wt(ve, "tmm", ttr)
                        wt(ve, "aev", aev)
                        wt(ve, "dst", trk["xf_st"][tb])
                        vv = inc(ve.tensor_copy(out=xf32[:, tb * H + q4 * 512: tb * H + (q4 + 1) * 512],
                                                in_=ps_tr[ip][:]), "vmi")
                        trk["tr_ev_v"][ip] = vv
                trk["xn_done"][tb] = trk["tr_mm"][(3) % 2]

                if own:
                    wt(sc, "vmi", trk["tr_ev_v"][1])
                    wt(sc, "vmi", trk["tr_ev_v"][0])
                    srcap = xf32[:, tb * H:(tb + 1) * H].rearrange("p (c u) -> p c u", c=HC)
                    dstap = xnf_s.rearrange("c p u -> p c u")[:, :, g * GT + t * 128: g * GT + (t + 1) * 128]
                    st = inc(sc.dma_start(out=dstap, in_=srcap), "dst", 16)
                    trk["xf_st"][tb] = st

            def wload(buf, src2d, a, pname, waitv):
                wt(gp, "tmm", waitv)
                bwidth = src2d.shape[1]
                return inc(gp.dma_start(out=buf[:, 0:a * bwidth].rearrange("p (a b) -> p a b", a=a),
                                        in_=src2d.rearrange("(a p) b -> p a b", p=128)), pname, 16)

            for g in range(ngrp):
                lwv = wload(wbufA, wkvd, HC, "dw0", trk["wA_mm"])
                for t in range(8):
                    ln_tile(g, t, False)
                xnfm_ready = cnt["aev"]
                if ln_only:
                    continue

                lku = wload(wbufB, wku, LC, "dw1", trk["wB_mm"])
                for l in range(LC):
                    ip = l % 2
                    wt(te, "aev", xnfm_ready)
                    wt(te, "dw0", lwv)
                    wt(te, "smi", trk["mm_ev"][ip])
                    for c in range(HC):
                        for hh in range(2):
                            ii = te.matmul(out=ps_mm[ip][:, hh * 512:(hh + 1) * 512],
                                           lhsT=wbufA[:, c * KVC + l * 128: c * KVC + (l + 1) * 128],
                                           rhs=xnfm[:, c * GT + hh * 512: c * GT + (hh + 1) * 512],
                                           start=(c == 0), stop=(c == HC - 1))
                    tv = inc(ii, "tmm")
                    if l >= LC - 2:
                        trk["wA_mm"] = tv
                    wt(sc, "tmm", tv)
                    ev = inc(sc.activation(out=kvcb[:, l * GT:(l + 1) * GT], in_=ps_mm[ip][:],
                                           func=AF.Identity, bias=c_bkvd[:, l:l + 1]), "smi")
                    trk["mm_ev"][ip] = ev
                kvc_smi = cnt["smi"]

                lkr = wload(wbufA, wkr, LC, "dw0", trk["wA_mm"])
                for h in range(NH):
                    ip = h % 2
                    wt(te, "smi", kvc_smi)
                    wt(te, "dw1", lku)
                    wt(te, "smi", trk["mm_ev"][ip])
                    for l in range(LC):
                        for hh in range(2):
                            ii = te.matmul(out=ps_mm[ip][:, hh * 512:(hh + 1) * 512],
                                           lhsT=wbufB[:, l * H + h * 128: l * H + (h + 1) * 128],
                                           rhs=kvcb[:, l * GT + hh * 512: l * GT + (hh + 1) * 512],
                                           start=(l == 0), stop=(l == LC - 1))
                    tv = inc(ii, "tmm")
                    if h >= NH - 2:
                        trk["wB_mm"] = tv
                    kb2 = h % 2
                    wt(sc, "tmm", tv)
                    wt(sc, "dst", trk["kst_st"][kb2])
                    wt(sc, "vmi", cnt.get("_kstv%d" % kb2, 0))
                    ev = inc(sc.activation(out=kst[:, kb2 * GT:(kb2 + 1) * GT], in_=ps_mm[ip][:],
                                           func=AF.Identity, bias=c_bku[:, h:h + 1]), "smi")
                    trk["mm_ev"][ip] = ev
                    st = inc(sc.dma_start(out=kc_s[h, :, g * GT:(g + 1) * GT],
                                          in_=kst[:, kb2 * GT:(kb2 + 1) * GT]), "dst", 16)
                    trk["kst_st"][kb2] = st
                    wt(ve, "smi", ev)
                    v1 = inc(ve.reduce_max(out=mred[:], in_=kst[:, kb2 * GT:(kb2 + 1) * GT],
                                           axis=AXX, apply_absolute_value=True), "vmi")
                    wt(ve, "vmi", v1)
                    v2 = inc(ve.tensor_tensor(out=p_M[:, h:h + 1], in0=p_M[:, h:h + 1],
                                              in1=mred[:], op=ALU.max), "vmi")
                    cnt["_kstv%d" % kb2] = v2

                lvu = wload(wbufB, wvu, LC, "dw1", trk["wB_mm"])
                wt(sy, "vmi", trk.get("cos_rd", 0))
                inc(sy.dma_start(out=cosb[:], in_=c4k[:, g * GT:(g + 1) * GT]), "dq", 16)
                dqc = inc(sy.dma_start(out=sinb[:], in_=s4k[:, g * GT:(g + 1) * GT]), "dq", 16)
                for p in range(8):
                    ip = p % 2
                    wt(te, "smi", kvc_smi)
                    wt(te, "dw0", lkr)
                    wt(te, "smi", trk["mm_ev"][ip])
                    for l in range(LC):
                        for hh in range(2):
                            ii = te.matmul(out=ps_mm[ip][:, hh * 512:(hh + 1) * 512],
                                           lhsT=wbufA[:, l * (NH * RD) + p * 128: l * (NH * RD) + (p + 1) * 128],
                                           rhs=kvcb[:, l * GT + hh * 512: l * GT + (hh + 1) * 512],
                                           start=(l == 0), stop=(l == LC - 1))
                    tv = inc(ii, "tmm")
                    if p >= 6:
                        trk["wA_mm"] = tv
                    wt(sc, "tmm", tv)
                    wt(sc, "vmi", trk["krw_v"])
                    evr = inc(sc.activation(out=krw[:], in_=ps_mm[ip][:],
                                            func=AF.Identity, bias=c_bkr[:, p:p + 1]), "smi")
                    trk["mm_ev"][ip] = evr
                    wt(te, "smi", evr)
                    wt(te, "smi", trk["psc_ev"])
                    for hh in range(2):
                        ii = te.matmul(out=ps_c[:, hh * 512:(hh + 1) * 512], lhsT=c_swp[:],
                                       rhs=krw[:, hh * 512:(hh + 1) * 512], start=True, stop=True)
                    tsw = inc(ii, "tmm")
                    wt(sc, "tmm", tsw)
                    evs = inc(sc.activation(out=krs[:], in_=ps_c[:], func=AF.Copy), "smi")
                    trk["psc_ev"] = evs
                    kb2 = p % 2
                    wt(ve, "smi", evs)
                    wt(ve, "dq", dqc)
                    v1 = inc(ve.tensor_tensor(out=km1[:], in0=krw[:], in1=cosb[:], op=ALU.mult), "vmi")
                    trk["krw_v"] = v1
                    wt(ve, "vmi", v1)
                    v2 = inc(ve.tensor_tensor(out=krs[:], in0=krs[:], in1=sinb[:], op=ALU.mult), "vmi")
                    trk["cos_rd"] = v2
                    wt(ve, "vmi", v2)
                    wt(ve, "dst", trk["kst_st"][kb2])
                    v3 = inc(ve.tensor_tensor(out=kst[:, kb2 * GT:(kb2 + 1) * GT], in0=km1[:],
                                              in1=krs[:], op=ALU.add), "vmi")
                    wt(ve, "vmi", v3)
                    v4 = inc(ve.reduce_max(out=mred[:], in_=kst[:, kb2 * GT:(kb2 + 1) * GT],
                                           axis=AXX, apply_absolute_value=True), "vmi")
                    wt(ve, "vmi", v4)
                    v5 = inc(ve.tensor_tensor(out=p_M[:, 16 + p:17 + p], in0=p_M[:, 16 + p:17 + p],
                                              in1=mred[:], op=ALU.max), "vmi")
                    cnt["_kstv%d" % kb2] = v5
                    wt(sc, "vmi", v3)
                    st = inc(sc.dma_start(out=kr_s[p, :, g * GT:(g + 1) * GT],
                                          in_=kst[:, kb2 * GT:(kb2 + 1) * GT]), "dst", 16)
                    trk["kst_st"][kb2] = st

                for st8 in range(8):
                    for dh in range(2):
                        ip = (st8 * 2 + dh) % 2
                        wt(te, "dw1", lvu)
                        wt(te, "smi", trk["mm_ev"][ip])
                        for l in range(LC):
                            for hh in range(2):
                                ii = te.matmul(out=ps_mm[ip][:, hh * 512:(hh + 1) * 512],
                                               lhsT=kvcb[:, l * GT + st8 * 128:l * GT + (st8 + 1) * 128],
                                               rhs=wbufB[:, l * H + dh * 1024 + hh * 512:
                                                         l * H + dh * 1024 + (hh + 1) * 512],
                                               start=(l == 0), stop=False)
                        for hh in range(2):
                            ii = te.matmul(out=ps_mm[ip][:, hh * 512:(hh + 1) * 512],
                                           lhsT=c_ones_r[:],
                                           rhs=c_bvu[0:1, dh * 1024 + hh * 512: dh * 1024 + (hh + 1) * 512],
                                           start=False, stop=True)
                        tv = inc(ii, "tmm")
                        kb2 = ip
                        wt(sc, "tmm", tv)
                        wt(sc, "dst", trk["kst_st"][kb2])
                        wt(sc, "vmi", cnt.get("_kstv%d" % kb2, 0))
                        ev = inc(sc.activation(out=kst[:, kb2 * GT:(kb2 + 1) * GT], in_=ps_mm[ip][:],
                                               func=AF.Copy), "smi")
                        trk["mm_ev"][ip] = ev
                        stv = inc(sc.dma_start(out=v_s[g * GT + st8 * 128: g * GT + (st8 + 1) * 128,
                                                       dh * 1024:(dh + 1) * 1024],
                                               in_=kst[:, kb2 * GT:(kb2 + 1) * GT]), "dst", 16)
                        trk["kst_st"][kb2] = stv
                trk["wB_mm"] = cnt["tmm"]

            for og in range(nown):
                for t in range(8):
                    ln_tile(og, t, True)
                xnfm_ready = cnt["aev"]
                if qstage < 1:
                    continue
                wt(sc, "aev", xnfm_ready)
                inc(sc.dma_start(out=xnb_s.rearrange("c p u -> p c u")[:, :, og * GT:(og + 1) * GT],
                                 in_=xnfm[:].rearrange("p (c u) -> p c u", c=HC)), "dst", 16)

                for oc in range(QCC if qstage >= 2 else 0):
                    buf = [wbufA, wbufB][oc % 2]
                    lw = wload(buf, wqd[:, oc * 128:(oc + 1) * 128], HC,
                               "dw0" if oc % 2 == 0 else "dw1",
                               trk["wA_mm"] if oc % 2 == 0 else trk["wB_mm"])
                    ip = oc % 2
                    wt(te, "aev", xnfm_ready)
                    wt(te, "dw0" if oc % 2 == 0 else "dw1", lw)
                    wt(te, "smi", trk["mm_ev"][ip])
                    for c in range(HC):
                        for hh in range(2):
                            ii = te.matmul(out=ps_mm[ip][:, hh * 512:(hh + 1) * 512],
                                           lhsT=buf[:, c * 128:(c + 1) * 128],
                                           rhs=xnfm[:, c * GT + hh * 512: c * GT + (hh + 1) * 512],
                                           start=(c == 0), stop=(c == HC - 1))
                    tv = inc(ii, "tmm")
                    if oc % 2 == 0:
                        trk["wA_mm"] = tv
                    else:
                        trk["wB_mm"] = tv
                    wt(sc, "tmm", tv)
                    wt(sc, "tmm", trk["qcg_smi"])
                    ev = inc(sc.activation(out=qcg[:, oc * GT:(oc + 1) * GT], in_=ps_mm[ip][:],
                                           func=AF.Identity, bias=c_bqd[:, oc:oc + 1]), "smi")
                    trk["mm_ev"][ip] = ev
                qcg_smi = cnt["smi"]

                for h in range(NH if qstage >= 3 else 0):
                    buf = [wbufA, wbufB][h % 2]
                    lw = wload(buf, wqu[:, h * 128:(h + 1) * 128], QCC,
                               "dw0" if h % 2 == 0 else "dw1",
                               trk["wA_mm"] if h % 2 == 0 else trk["wB_mm"])
                    ip = h % 2
                    wt(te, "smi", qcg_smi)
                    wt(te, "dw0" if h % 2 == 0 else "dw1", lw)
                    wt(te, "smi", trk["mm_ev"][ip])
                    for l in range(QCC):
                        for hh in range(2):
                            ii = te.matmul(out=ps_mm[ip][:, hh * 512:(hh + 1) * 512],
                                           lhsT=buf[:, l * 128:(l + 1) * 128],
                                           rhs=qcg[:, l * GT + hh * 512: l * GT + (hh + 1) * 512],
                                           start=(l == 0), stop=(l == QCC - 1))
                    tv = inc(ii, "tmm")
                    if h % 2 == 0:
                        trk["wA_mm"] = tv
                    else:
                        trk["wB_mm"] = tv
                    kb2 = h % 2
                    wt(sc, "tmm", tv)
                    wt(sc, "dst", trk["kst_st"][kb2])
                    wt(sc, "vmi", cnt.get("_kstv%d" % kb2, 0))
                    ev = inc(sc.activation(out=kst[:, kb2 * GT:(kb2 + 1) * GT], in_=ps_mm[ip][:],
                                           func=AF.Identity, bias=c_bqu[:, h:h + 1]), "smi")
                    trk["mm_ev"][ip] = ev
                    stv = inc(sc.dma_start(out=qc_sc[h, :, og * GT:(og + 1) * GT],
                                           in_=kst[:, kb2 * GT:(kb2 + 1) * GT]), "dst", 16)
                    trk["kst_st"][kb2] = stv

                wt(sy, "vmi", trk.get("cos_rd", 0))
                inc(sy.dma_start(out=cosb[:], in_=c4q[:, og * GT:(og + 1) * GT]), "dq", 16)
                dqc = inc(sy.dma_start(out=sinb[:], in_=s4q[:, og * GT:(og + 1) * GT]), "dq", 16)
                for p in range(8 if qstage >= 4 else 0):
                    buf = [wbufA, wbufB][p % 2]
                    lw = wload(buf, wqr[:, p * 128:(p + 1) * 128], QCC,
                               "dw0" if p % 2 == 0 else "dw1",
                               trk["wA_mm"] if p % 2 == 0 else trk["wB_mm"])
                    ip = p % 2
                    wt(te, "smi", qcg_smi)
                    wt(te, "dw0" if p % 2 == 0 else "dw1", lw)
                    wt(te, "smi", trk["mm_ev"][ip])
                    for l in range(QCC):
                        for hh in range(2):
                            ii = te.matmul(out=ps_mm[ip][:, hh * 512:(hh + 1) * 512],
                                           lhsT=buf[:, l * 128:(l + 1) * 128],
                                           rhs=qcg[:, l * GT + hh * 512: l * GT + (hh + 1) * 512],
                                           start=(l == 0), stop=(l == QCC - 1))
                    tv = inc(ii, "tmm")
                    if p % 2 == 0:
                        trk["wA_mm"] = tv
                    else:
                        trk["wB_mm"] = tv
                    wt(sc, "tmm", tv)
                    wt(sc, "vmi", trk["krw_v"])
                    evr = inc(sc.activation(out=krw[:], in_=ps_mm[ip][:],
                                            func=AF.Identity, bias=c_bqr[:, p:p + 1]), "smi")
                    trk["mm_ev"][ip] = evr
                    wt(te, "smi", evr)
                    wt(te, "smi", trk["psc_ev"])
                    for hh in range(2):
                        ii = te.matmul(out=ps_c[:, hh * 512:(hh + 1) * 512], lhsT=c_swp[:],
                                       rhs=krw[:, hh * 512:(hh + 1) * 512], start=True, stop=True)
                    tsw = inc(ii, "tmm")
                    wt(sc, "tmm", tsw)
                    evs = inc(sc.activation(out=krs[:], in_=ps_c[:], func=AF.Copy), "smi")
                    trk["psc_ev"] = evs
                    kb2 = p % 2
                    wt(ve, "smi", evs)
                    wt(ve, "dq", dqc)
                    v1 = inc(ve.tensor_tensor(out=km1[:], in0=krw[:], in1=cosb[:], op=ALU.mult), "vmi")
                    trk["krw_v"] = v1
                    wt(ve, "vmi", v1)
                    v2 = inc(ve.tensor_tensor(out=krs[:], in0=krs[:], in1=sinb[:], op=ALU.mult), "vmi")
                    trk["cos_rd"] = v2
                    wt(ve, "vmi", v2)
                    wt(ve, "dst", trk["kst_st"][kb2])
                    v3 = inc(ve.tensor_tensor(out=kst[:, kb2 * GT:(kb2 + 1) * GT], in0=km1[:],
                                              in1=krs[:], op=ALU.add), "vmi")
                    wt(sc, "vmi", v3)
                    stv = inc(sc.dma_start(out=qr_sc[p, :, og * GT:(og + 1) * GT],
                                           in_=kst[:, kb2 * GT:(kb2 + 1) * GT]), "dst", 16)
                    trk["kst_st"][kb2] = stv
                trk["qcg_smi"] = cnt["tmm"]

            wt(sc, "dst", cnt["dst"])
            barrier()
            p1s.close()

            for p2s in ([contextlib.ExitStack()] if phases >= 2 else []):
                _e = p2s.enter_context
                c_dneg = _e(nc.sbuf_tensor("c_dneg", [128, 4 * 512], BF16))
                c_flgi = _e(nc.sbuf_tensor("c_flgi", [128, 16 * 128], BF16))
                kcb0 = _e(nc.sbuf_tensor("kcb0", [128, 3 * 512], BF16))
                kcb1 = _e(nc.sbuf_tensor("kcb1", [128, 3 * 512], BF16))
                krb = _e(nc.sbuf_tensor("krb", [128, 3 * 512], BF16))
                vb0 = _e(nc.sbuf_tensor("vb0", [128, 3 * 512], BF16))
                vb1 = _e(nc.sbuf_tensor("vb1", [128, 3 * 512], BF16))
                qcb = _e(nc.sbuf_tensor("qcb", [128, 2 * 512], BF16))
                qrb = _e(nc.sbuf_tensor("qrb", [128, 512], BF16))
                aqc = _e(nc.sbuf_tensor("aqc", [128, 2 * 512], BF16))
                aqr = _e(nc.sbuf_tensor("aqr", [128, 512], BF16))
                prb = _e(nc.sbuf_tensor("prb", [128, 2 * 5 * 512], BF16))
                cst = _e(nc.sbuf_tensor("cst", [128, 2 * 512], BF16))
                dnvb = _e(nc.sbuf_tensor("dnvb", [1, 2 * 512], BF16))
                dbc = _e(nc.sbuf_tensor("dbc", [128, 2 * 512], F32))
                bsk = _e(nc.sbuf_tensor("bsk", [128, 2 * 16], F32))
                mx1 = _e(nc.sbuf_tensor("mx1", [1, 2], BF16))
                scp = [[_e(nc.psum_tensor("sc%d_%d" % (hx, j), [128, 512], F32)) for j in range(2)]
                       for hx in range(2)]
                cxp = [_e(nc.psum_tensor("cx%d" % hx, [128, 512], F32)) for hx in range(2)]
                dnp = [_e(nc.psum_tensor("dn%d" % hx, [1, 512], F32)) for hx in range(2)]
                kcb = [kcb0, kcb1]
                vbb = [vb0, vb1]

                inc(sy.dma_start(out=c_dneg[:].rearrange("p (a b) -> p a b", a=4),
                                 in_=dneg.rearrange("a p b -> p a b")), "dq", 16)
                dq_cst = inc(sy.dma_start(out=c_flgi[:].rearrange("p (a b) -> p a b", a=16),
                                          in_=flgi.rearrange("s c p f -> p (s c) f")), "dq", 16)
                wt(te, "dq", dq_cst)

                t2 = {
                    "q_rd": 0,
                    "q_ab": 0,
                    "pair_tctx": 0,
                    "pair_aex": 0,
                    "dn_free": [0, 0],
                    "cx_free": [0, 0],
                    "sc0_smi": [0, 0],
                    "dbc_rd": [0, 0],
                    "cst_st": [0, 0],
                    "bsk_rd": 0,
                    "mhat_rd": 0,
                }

                for slot in range(NSLOT):
                    for pr in range(8):
                        h0 = 2 * pr
                        wt(sy, "tsc", t2["q_rd"])
                        wt(sy, "smi", t2["q_ab"])
                        inc(sy.dma_start(out=qcb[:, 0:512], in_=qc_sc[h0, :, slot * 512:(slot + 1) * 512]), "dq", 16)
                        inc(sy.dma_start(out=qcb[:, 512:1024], in_=qc_sc[h0 + 1, :, slot * 512:(slot + 1) * 512]), "dq", 16)
                        dqv = inc(sy.dma_start(out=qrb[:], in_=qr_sc[pr, :, slot * 512:(slot + 1) * 512]), "dq", 16)
                        wt(sc, "dq", dqv)
                        inc(sc.activation(out=aqc[:], in_=qcb[:], func=AF.Abs), "smi")
                        a2 = inc(sc.activation(out=aqr[:], in_=qrb[:], func=AF.Abs), "smi")
                        t2["q_ab"] = a2
                        vbk = [0, 0]
                        for hx in range(2):
                            base = hx * 64
                            col = slot * 16 + h0 + hx
                            wt(te, "smi", a2)
                            wt(te, "vmi", t2["dn_free"][hx])
                            te.matmul(out=dnp[hx][:], lhsT=p_M[:, h0 + hx:h0 + hx + 1],
                                      rhs=aqc[:, hx * 512:(hx + 1) * 512], start=True, stop=False)
                            ii = te.matmul(out=dnp[hx][:], lhsT=p_M[base:base + 64, 16 + pr:17 + pr],
                                           rhs=aqr[base:base + 64, :], start=False, stop=True)
                            tpm = inc(ii, "tmm")
                            wt(ve, "tmm", tpm)
                            vmx = inc(ve.reduce_max(out=mx1[0:1, hx:hx + 1], in_=dnp[hx][:], axis=AXX), "vmi")
                            t2["dn_free"][hx] = vmx
                            wt(te, "vmi", vmx)
                            wt(te, "smi", t2["sc0_smi"][hx])
                            wt(te, "aex", t2["pair_aex"])
                            ii = te.matmul(out=scp[hx][0][0:128, 0:1], lhsT=c_ones_r[:],
                                           rhs=mx1[0:1, hx:hx + 1], start=True, stop=True)
                            tpb = inc(ii, "tmm")
                            wt(sc, "tmm", tpb)
                            wt(sc, "vmi", t2["mhat_rd"])
                            evm = inc(sc.activation(out=p_mhat[:, col:col + 1], in_=scp[hx][0][0:128, 0:1],
                                                    func=AF.Identity, scale=-SCALE), "smi")
                            t2["sc0_smi"][hx] = evm
                            wt(ve, "smi", evm)
                            wt(ve, "aex", t2["bsk_rd"])
                            vbk[hx] = inc(ve.tensor_scalar(out=bsk[:, hx * 16:(hx + 1) * 16],
                                                           in0=c_bflag[:, slot * 16:(slot + 1) * 16],
                                                           scalar1=p_mhat[:, col:col + 1], scalar2=None,
                                                           op0=ALU.add), "vmi")
                            t2["mhat_rd"] = vbk[hx]

                        LEN = SLOT_LEN[slot]
                        nsub = LEN * 4
                        kload = [("dk0", 0), ("dk0", 0), ("dk0", 0)]
                        aexL = [[], []]
                        tscL = [[], []]
                        ctxL = []
                        dnL = {}

                        def load_kb(kb):
                            kb3 = kb % 3
                            sem = "dk0" if kb % 2 == 0 else "dk1"
                            wt(sy, "tctx", dnL.get(kb - 3, t2["pair_tctx"]))
                            inc(sy.dma_start(out=kcb[0][:, kb3 * 512:(kb3 + 1) * 512],
                                             in_=kc_s[h0, :, kb * 512:(kb + 1) * 512]), sem, 16)
                            inc(sy.dma_start(out=kcb[1][:, kb3 * 512:(kb3 + 1) * 512],
                                             in_=kc_s[h0 + 1, :, kb * 512:(kb + 1) * 512]), sem, 16)
                            inc(sy.dma_start(out=krb[:, kb3 * 512:(kb3 + 1) * 512],
                                             in_=kr_s[pr, :, kb * 512:(kb + 1) * 512]), sem, 16)
                            for hx in range(2):
                                vv = inc(
                                    sy.dma_start(out=vbb[hx][:, kb3 * 512:(kb3 + 1) * 512].rearrange(
                                                     "p (a b) -> p a b", a=4),
                                                 in_=v_s[kb * 512:(kb + 1) * 512,
                                                         (h0 + hx) * 128:(h0 + hx + 1) * 128].rearrange(
                                                     "(a p) b -> p a b", p=128)), sem, 16)
                            kload[kb3] = (sem, vv)

                        load_kb(0)
                        if LEN > 1:
                            load_kb(1)
                        if LEN > 2:
                            load_kb(2)

                        def em_scores(i):
                            kb, ksub = i // 4, i % 4
                            kbp = kb % 3
                            ip = i % 2
                            cand = kb >= LEN - 4
                            wt(te, kload[kbp][0], kload[kbp][1])
                            for hx in range(2):
                                if i >= 2:
                                    wt(te, "aex", aexL[hx][i - 2])
                                else:
                                    wt(te, "aex", t2["pair_aex"])
                                    if ip == 0:
                                        wt(te, "smi", t2["sc0_smi"][hx])
                                if i == 0:
                                    wt(te, "vmi", vbk[hx])
                            for hx in range(2):
                                te.matmul(out=scp[hx][ip][:],
                                          lhsT=kcb[hx][:, kbp * 512 + ksub * 128: kbp * 512 + (ksub + 1) * 128],
                                          rhs=qcb[:, hx * 512:(hx + 1) * 512], start=True, stop=False)
                            iis = [None, None]
                            for hx in range(2):
                                base = hx * 64
                                iis[hx] = te.matmul(out=scp[hx][ip][:],
                                                    lhsT=krb[base:base + 64, kbp * 512 + ksub * 128:
                                                             kbp * 512 + (ksub + 1) * 128],
                                                    rhs=qrb[base:base + 64, :], start=False, stop=not cand)
                            if cand:
                                ci = kb - (LEN - 4)
                                for hx in range(2):
                                    iis[hx] = te.matmul(out=scp[hx][ip][:],
                                                        lhsT=c_flgi[:, (slot * 4 + ci) * 128:(slot * 4 + ci + 1) * 128],
                                                        rhs=c_dneg[:, ksub * 512:(ksub + 1) * 512],
                                                        start=False, stop=True)
                            for hx in range(2):
                                tscL[hx].append(inc(iis[hx], "tsc"))

                        def em_exp(i):
                            kb = i // 4
                            ip = i % 2
                            s5 = i % 5
                            for hx in range(2):
                                wt(sc, "tsc", tscL[hx][i])
                                if i >= 5:
                                    kbp5 = (i - 5) // 4
                                    wt(sc, "tctx", max(ctxL[i - 5], dnL.get(kbp5, 0)))
                                else:
                                    wt(sc, "tctx", t2["pair_tctx"])
                                if i == 0:
                                    wt(sc, "vmi", vbk[hx])
                                aexL[hx].append(inc(sc.activation(
                                    out=prb[:, (hx * 5 + s5) * 512:(hx * 5 + s5 + 1) * 512],
                                    in_=scp[hx][ip][:], func=AF.Exp,
                                    bias=bsk[:, hx * 16 + kb:hx * 16 + kb + 1], scale=SCALE), "aex"))

                        def em_ctx(i):
                            kb, ksub = i // 4, i % 4
                            kbp = kb % 3
                            s5 = i % 5
                            st_ = (i == 0)
                            sp_ = (i == nsub - 1)
                            for hx in range(2):
                                wt(te, "aex", aexL[hx][i])
                                if i == 0:
                                    wt(te, "vmi", t2["cx_free"][hx])
                                ii = te.matmul(out=cxp[hx][:],
                                               lhsT=vbb[hx][:, kbp * 512 + ksub * 128: kbp * 512 + (ksub + 1) * 128],
                                               rhs=prb[:, (hx * 5 + s5) * 512:(hx * 5 + s5 + 1) * 512],
                                               start=st_, stop=sp_)
                            ctxL.append(inc(ii, "tctx"))

                        def em_denoms(kb):
                            for hx in range(2):
                                for ksub in range(4):
                                    i = kb * 4 + ksub
                                    s5 = i % 5
                                    ii = te.matmul(out=dnp[hx][:], lhsT=c_ones_c[:],
                                                   rhs=prb[:, (hx * 5 + s5) * 512:(hx * 5 + s5 + 1) * 512],
                                                   start=(kb == 0 and ksub == 0),
                                                   stop=(kb == LEN - 1 and ksub == 3))
                            dnL[kb] = inc(ii, "tctx")

                        for i in range(nsub + 1):
                            if i < nsub:
                                em_scores(i)
                                em_exp(i)
                            if i >= 1:
                                em_ctx(i - 1)
                                if (i - 1) % 4 == 3:
                                    kbdone = (i - 1) // 4
                                    em_denoms(kbdone)
                                    if kbdone + 3 < LEN:
                                        load_kb(kbdone + 3)
                        t2["q_rd"] = max(tscL[0][-1], tscL[1][-1])

                        tlast = cnt["tctx"]
                        for hx in range(2):
                            wt(ve, "tctx", tlast)
                            with nc.allow_low_precision(reason="bf16 softmax denominators (common-factor exact)"):
                                vrc = inc(ve.reciprocal(out=dnvb[0:1, hx * 512:(hx + 1) * 512],
                                                        in_=dnp[hx][:]), "vmi")
                            t2["dn_free"][hx] = vrc
                            wt(te, "vmi", vrc)
                            wt(te, "aex", aexL[hx][-1])
                            wt(te, "smi", t2["sc0_smi"][hx])
                            ii = te.matmul(out=scp[hx][0][:], lhsT=c_ones_r[:],
                                           rhs=dnvb[0:1, hx * 512:(hx + 1) * 512], start=True, stop=True)
                            tdb = inc(ii, "tmm")
                            wt(sc, "tmm", tdb)
                            wt(sc, "vmi", t2["dbc_rd"][hx])
                            evd = inc(sc.activation(out=dbc[:, hx * 512:(hx + 1) * 512],
                                                    in_=scp[hx][0][:], func=AF.Copy), "smi")
                            t2["sc0_smi"][hx] = evd
                            wt(ve, "smi", evd)
                            wt(ve, "dst", t2["cst_st"][hx])
                            vct = inc(ve.tensor_tensor(out=cst[:, hx * 512:(hx + 1) * 512],
                                                       in0=cxp[hx][:], in1=dbc[:, hx * 512:(hx + 1) * 512],
                                                       op=ALU.mult), "vmi")
                            t2["cx_free"][hx] = vct
                            t2["dbc_rd"][hx] = vct
                            wt(sc, "vmi", vct)
                            t2["cst_st"][hx] = inc(sc.dma_start(out=ctx_s[h0 + hx, :, slot * 512:(slot + 1) * 512],
                                                                in_=cst[:, hx * 512:(hx + 1) * 512]), "dst", 16)
                        t2["bsk_rd"] = cnt["aex"]
                        t2["pair_aex"] = cnt["aex"]
                        t2["pair_tctx"] = cnt["tctx"]

                wt(sc, "dst", cnt["dst"])
                barrier()
                p2s.close()

            for rts in ([contextlib.ExitStack()] if phases >= 3 else []):
                _e = rts.enter_context
                gw_sb = _e(nc.sbuf_tensor("gw_sb", [128, HC * 32], F32))
                xfb = _e(nc.sbuf_tensor("xfb", [128, 2 * H], F32))
                lgsb = _e(nc.sbuf_tensor("lgsb", [32, TOK], F32))
                ltt = _e(nc.sbuf_tensor("ltt", [128, 32], F32))
                top8 = _e(nc.sbuf_tensor("top8", [128, 8], F32))
                wrk = _e(nc.sbuf_tensor("wrk", [128, 32], F32))
                selb = _e(nc.sbuf_tensor("selb", [128, 32], F32))
                negm = _e(nc.sbuf_tensor("negm", [128, 1], F32))
                pex = _e(nc.sbuf_tensor("pex", [128, 32], F32))
                den8 = _e(nc.sbuf_tensor("den8", [128, 1], F32))
                p_mwt = _e(nc.sbuf_tensor("p_mwt", [32, TOK], F32))
                lg = [_e(nc.psum_tensor("lg%d" % i, [32, 512], F32)) for i in range(4)]
                ptt = _e(nc.psum_tensor("ptt", [128, 32], F32))
                pmw = _e(nc.psum_tensor("pmw", [32, 128], F32))
                dgw = inc(sy.dma_start(out=gw_sb[:].rearrange("p (a b) -> p a b", a=HC),
                                       in_=gwt.rearrange("(a p) b -> p a b", p=128)), "dq", 16)
                xf_rd = [0, 0]
                for c in range(HC):
                    cb = c % 2
                    wt(sy, "tmm", xf_rd[cb])
                    dxl = inc(sy.dma_start(out=xfb[:, cb * H:(cb + 1) * H], in_=xnf_s[c]), "dq", 16)
                    wt(te, "dq", dxl)
                    if c == 0:
                        wt(te, "dq", dgw)
                    for tb4 in range(4):
                        ii = te.matmul(out=lg[tb4][:], lhsT=gw_sb[:, c * 32:(c + 1) * 32],
                                       rhs=xfb[:, cb * H + tb4 * 512: cb * H + (tb4 + 1) * 512],
                                       start=(c == 0), stop=(c == HC - 1))
                    xf_rd[cb] = inc(ii, "tmm")
                tlg = cnt["tmm"]
                wt(sc, "tmm", tlg)
                for tb4 in range(4):
                    ev = inc(sc.activation(out=lgsb[:, tb4 * 512:(tb4 + 1) * 512], in_=lg[tb4][:],
                                           func=AF.Copy), "smi")
                lg_ev = cnt["smi"]

                ptt_rd = 0
                pmw_ev = 0
                for tt in range(16):
                    wt(te, "smi", lg_ev)
                    wt(te, "smi", ptt_rd)
                    ii = te.transpose(out=ptt[:], in_=lgsb[:, tt * 128:(tt + 1) * 128],
                                      identity=c_identf[0:32, 0:32])
                    tv = inc(ii, "tmm")
                    wt(sc, "tmm", tv)
                    ev = inc(sc.activation(out=ltt[:], in_=ptt[:], func=AF.Copy), "smi")
                    ptt_rd = ev
                    wt(ve, "smi", ev)
                    v1 = inc(ve.max(out=top8[:], in_=ltt[:]), "vmi")
                    wt(ve, "vmi", v1)
                    v2 = inc(ve.match_replace(out=wrk[:], in_to_replace=top8[:], in_values=ltt[:],
                                              imm_value=-1.0e30), "vmi")
                    wt(ve, "vmi", v2)
                    v3 = inc(ve.tensor_tensor(out=selb[:], in0=wrk[:], in1=ltt[:], op=ALU.not_equal), "vmi")
                    wt(ve, "vmi", v3)
                    v4 = inc(ve.tensor_scalar(out=negm[:], in0=top8[:, 0:1], scalar1=-1.0,
                                              scalar2=None, op0=ALU.mult), "vmi")
                    wt(sc, "vmi", v4)
                    s1 = inc(sc.activation(out=pex[:], in_=ltt[:], func=AF.Exp,
                                           bias=negm[:], scale=1.0), "smi")
                    wt(ve, "smi", s1)
                    v5 = inc(ve.tensor_tensor(out=pex[:], in0=pex[:], in1=selb[:], op=ALU.mult), "vmi")
                    wt(ve, "vmi", v5)
                    v6 = inc(ve.reduce_sum(out=den8[:], in_=pex[:], axis=AXX), "vmi")
                    wt(ve, "vmi", v6)
                    v7 = inc(ve.reciprocal(out=den8[:], in_=den8[:]), "vmi")
                    wt(ve, "vmi", v7)
                    v8 = inc(ve.tensor_scalar(out=pex[:], in0=pex[:], scalar1=den8[:],
                                              scalar2=None, op0=ALU.mult), "vmi")
                    wt(te, "vmi", v8)
                    wt(te, "smi", pmw_ev)
                    ii = te.transpose(out=pmw[:], in_=pex[:], identity=c_identf[:])
                    tv2 = inc(ii, "tmm")
                    wt(sc, "tmm", tv2)
                    pmw_ev = inc(sc.activation(out=p_mwt[0:32, tt * 128:(tt + 1) * 128],
                                               in_=pmw[:], func=AF.Copy), "smi")
                wt(ve, "smi", pmw_ev)
                inc(ve.tensor_copy(out=p_mwtb[:], in_=p_mwt[:]), "vmi")
                if debug:
                    wt(sc, "smi", pmw_ev)
                    inc(sc.dma_start(out=dbgm_s, in_=p_mwt[:]), "dst", 16)
                barrier()
                rts.close()

            for hf in range(2 if phases >= 3 else 0):
                with contextlib.ExitStack() as hfs:
                    _e = hfs.enter_context
                    acc = _e(nc.sbuf_tensor("acc%d" % hf, [128, HC * GT], F32))
                    wo_scope = contextlib.ExitStack()
                    ctxsb = wo_scope.enter_context(nc.sbuf_tensor("ctxsb%d" % hf, [128, NH * GT], BF16))
                    wob = wo_scope.enter_context(nc.sbuf_tensor("wob%d" % hf, [128, 2 * 2048], BF16))
                    moe_psums = contextlib.ExitStack()
                    pG = [moe_psums.enter_context(nc.psum_tensor("pG%d_%d" % (i, hf), [128, 1024], F32))
                          for i in range(2)]
                    pU = [moe_psums.enter_context(nc.psum_tensor("pU%d_%d" % (i, hf), [128, 1024], F32))
                          for i in range(2)]

                    wo_mm = [0, 0]
                    wo_ev_smi = [0, 0]
                    for dvh in range(NH):
                        inc(sy.dma_start(out=ctxsb[:, dvh * GT:(dvh + 1) * GT],
                                         in_=ctx_s[dvh, :, hf * GT:(hf + 1) * GT]), "dx", 16)
                    dctx = cnt["dx"]
                    for oc in range(HC):
                        ob = oc % 2
                        wt(gp, "tmm", wo_mm[ob])
                        lw = inc(gp.dma_start(out=wob[:, ob * 2048:(ob + 1) * 2048].rearrange(
                                                  "p (a b) -> p a b", a=HC),
                                              in_=wo[:, oc * 128:(oc + 1) * 128].rearrange(
                                                  "(a p) b -> p a b", p=128)),
                                 "dw0", 16)
                        wt(te, "dw0", lw)
                        wt(te, "dx", dctx)
                        wt(te, "smi", wo_ev_smi[ob])
                        for dvh in range(NH):
                            for hh in range(2):
                                ii = te.matmul(out=pG[ob][:, hh * 512:(hh + 1) * 512],
                                               lhsT=wob[:, ob * 2048 + dvh * 128: ob * 2048 + (dvh + 1) * 128],
                                               rhs=ctxsb[:, dvh * GT + hh * 512: dvh * GT + (hh + 1) * 512],
                                               start=(dvh == 0), stop=(dvh == NH - 1))
                        tv = inc(ii, "tmm")
                        wo_mm[ob] = tv
                        wt(sc, "tmm", tv)
                        ev = inc(sc.activation(out=acc[:, oc * GT:(oc + 1) * GT], in_=pG[ob][:],
                                               func=AF.Identity, bias=c_bo[:, oc:oc + 1]), "smi")
                        wo_ev_smi[ob] = ev
                    acc_init = cnt["smi"]
                    acc_init_v = 0
                    if debug:
                        wt(sc, "smi", acc_init)
                        wt(sc, "vmi", acc_init_v)
                        inc(sc.dma_start(out=dbga_s[hf], in_=acc[:]), "dst", 16)
                    wo_scope.close()
                    moe_sb = contextlib.ExitStack()
                    _em = moe_sb.enter_context
                    xnm = _em(nc.sbuf_tensor("xnm%d" % hf, [128, HC * GT], BF16))
                    wgb = _em(nc.sbuf_tensor("wgb%d" % hf, [128, HC * NI], BF16))
                    wub = _em(nc.sbuf_tensor("wub%d" % hf, [128, HC * NI], BF16))
                    wdb = _em(nc.sbuf_tensor("wdb%d" % hf, [128, ICC * H], BF16))
                    mwsb = _em(nc.sbuf_tensor("mwsb%d" % hf, [128, GT], F32))
                    umb = _em(nc.sbuf_tensor("umb%d" % hf, [128, GT], BF16))
                    actb = _em(nc.sbuf_tensor("actb%d" % hf, [128, ICC * GT], BF16))
                    wt(sy, "tmm", max(wo_mm))
                    wt(sy, "smi", max(wo_ev_smi))
                    for c in range(HC):
                        inc(sy.dma_start(out=xnm[:, c * GT:(c + 1) * GT],
                                         in_=xnb_s[c, :, hf * GT:(hf + 1) * GT]), "dx", 16)
                    dld = cnt["dx"]

                    m3 = {
                        "wg_mm": 0, "wu_mm": 0, "wd_mm": 0,
                        "g_ev": [0, 0], "u_ev": [0, 0],
                        "g_ev_v": [0, 0],
                        "d_ev": [0, 0],
                        "mw_ev": 0,
                        "sil_rd": 0, "um_rd": 0,
                        "act_rd": 0,
                        "acc_v": [0] * HC,
                    }
                    lgw = luw = ldw = 0
                    wt(gp, "tmm", max(wo_mm))
                    wt(gp, "smi", max(wo_ev_smi))
                    for e in range(NE):
                        wt(gp, "tmm", m3["wg_mm"])
                        lgw = inc(gp.dma_start(out=wgb[:].rearrange("p (a b) -> p a b", a=HC),
                                               in_=wg_t[e].rearrange("(a p) b -> p a b", p=128)), "dw0", 16)
                        wt(gp, "tmm", m3["wu_mm"])
                        luw = inc(gp.dma_start(out=wub[:].rearrange("p (a b) -> p a b", a=HC),
                                               in_=wu_t[e].rearrange("(a p) b -> p a b", p=128)), "dw0", 16)
                        wt(gp, "tctx", m3["wd_mm"])
                        ldw = inc(gp.dma_start(out=wdb[:].rearrange("p (a b) -> p a b", a=ICC),
                                               in_=wd_t[e].rearrange("(a p) b -> p a b", p=128)), "dw1", 16)

                        wt(te, "smi", m3["g_ev"][0])
                        wt(te, "vmi", m3["g_ev_v"][0])
                        m3["g_ev_v"][0] = 0
                        for hh in range(2):
                            ii = te.matmul(out=pG[0][:, hh * 512:(hh + 1) * 512],
                                           lhsT=c_oneh[0:32, e * 128:(e + 1) * 128],
                                           rhs=p_mwtb[0:32, hf * GT + hh * 512: hf * GT + (hh + 1) * 512],
                                           start=True, stop=True)
                        tmw = inc(ii, "tmm")
                        wt(sc, "tmm", tmw)
                        wt(sc, "vmi", m3["um_rd"])
                        mwe = inc(sc.activation(out=mwsb[:], in_=pG[0][:], func=AF.Copy), "smi")
                        m3["g_ev"][0] = mwe
                        m3["mw_ev"] = mwe

                        for ic in range(ICC):
                            ib = ic % 2
                            wt(te, "dw0", luw)
                            wt(te, "dx", dld)
                            wt(te, "smi", m3["g_ev"][ib])
                            wt(te, "vmi", m3["g_ev_v"][ib])
                            m3["g_ev_v"][ib] = 0
                            for c in range(HC):
                                for hh in range(2):
                                    ii = te.matmul(out=pG[ib][:, hh * 512:(hh + 1) * 512],
                                                   lhsT=wgb[:, c * NI + ic * 128: c * NI + (ic + 1) * 128],
                                                   rhs=xnm[:, c * GT + hh * 512: c * GT + (hh + 1) * 512],
                                                   start=(c == 0), stop=(c == HC - 1))
                            tg = inc(ii, "tmm")
                            if ic >= ICC - 2:
                                m3["wg_mm"] = tg
                            wt(te, "vmi", m3["u_ev"][ib])
                            for c in range(HC):
                                for hh in range(2):
                                    ii = te.matmul(out=pU[ib][:, hh * 512:(hh + 1) * 512],
                                                   lhsT=wub[:, c * NI + ic * 128: c * NI + (ic + 1) * 128],
                                                   rhs=xnm[:, c * GT + hh * 512: c * GT + (hh + 1) * 512],
                                                   start=(c == 0), stop=(c == HC - 1))
                            tu = inc(ii, "tmm")
                            if ic >= ICC - 2:
                                m3["wu_mm"] = tu
                            wt(sc, "tmm", tg)
                            wt(sc, "tctx", m3["act_rd"])
                            wt(sc, "vmi", m3["sil_rd"])
                            sv = inc(sc.activation(out=actb[:, ic * GT:(ic + 1) * GT],
                                                   in_=pG[ib][:], func=AF.Silu), "smi")
                            m3["g_ev"][ib] = sv
                            wt(ve, "tmm", tu)
                            wt(ve, "smi", mwe)
                            wt(ve, "vmi", m3["um_rd"])
                            uv = inc(ve.tensor_tensor(out=umb[:], in0=pU[ib][:], in1=mwsb[:],
                                                      op=ALU.mult), "vmi")
                            m3["u_ev"][ib] = uv
                            wt(ve, "smi", sv)
                            wt(ve, "vmi", uv)
                            av = inc(ve.tensor_tensor(out=actb[:, ic * GT:(ic + 1) * GT],
                                                      in0=actb[:, ic * GT:(ic + 1) * GT],
                                                      in1=umb[:], op=ALU.mult), "vmi")
                            m3["sil_rd"] = av
                            m3["um_rd"] = av
                        act_v = cnt["vmi"]

                        for hc in range(HC):
                            db = hc % 2
                            wt(te, "vmi", act_v)
                            wt(te, "dw1", ldw)
                            wt(te, "vmi", m3["d_ev"][db])
                            wt(te, "vmi", m3["u_ev"][db])
                            for ic in range(ICC):
                                for hh in range(2):
                                    ii = te.matmul(out=pU[db][:, hh * 512:(hh + 1) * 512],
                                                   lhsT=wdb[:, ic * H + hc * 128: ic * H + (hc + 1) * 128],
                                                   rhs=actb[:, ic * GT + hh * 512: ic * GT + (hh + 1) * 512],
                                                   start=(ic == 0), stop=(ic == ICC - 1))
                            td = inc(ii, "tctx")
                            if hc == HC - 1:
                                m3["wd_mm"] = td
                            wt(ve, "tctx", td)
                            if e == 0:
                                wt(ve, "smi", acc_init)
                                wt(ve, "vmi", acc_init_v)
                            wt(ve, "vmi", m3["acc_v"][hc])
                            avv = inc(ve.tensor_tensor(out=acc[:, hc * GT:(hc + 1) * GT],
                                                       in0=pU[db][:], in1=acc[:, hc * GT:(hc + 1) * GT],
                                                       op=ALU.add), "vmi")
                            m3["acc_v"][hc] = avv
                            m3["d_ev"][db] = avv
                            m3["u_ev"][db] = avv
                        m3["act_rd"] = cnt["tctx"]
                    moe_done = cnt["vmi"]
                    if debug:
                        wt(sc, "vmi", moe_done)
                        inc(sc.dma_start(out=dbgb_s[hf], in_=acc[:]), "dst", 16)
                    moe_psums.close()
                    moe_sb.close()

                    with contextlib.ExitStack() as fns:
                        _e = fns.enter_context
                        xtm = _e(nc.sbuf_tensor("xtm%d" % hf, [128, 2 * H], F32))
                        osb = _e(nc.sbuf_tensor("osb%d" % hf, [128, 2 * H], F32))
                        ps_f = [_e(nc.psum_tensor("pf%d_%d" % (i, hf), [128, 512], F32)) for i in range(2)]
                        fin = {"tr_mm": [0, 0], "tr_rd": [0, 0], "o_st": [0, 0], "x_rd": [0, 0]}
                        wt(sy, "vmi", moe_done)
                        for st8 in range(8):
                            tb = st8 % 2
                            wt(sy, "vmi", fin["x_rd"][tb])
                            dxo = inc(sy.dma_start(out=xtm[:, tb * H:(tb + 1) * H],
                                                   in_=xown[hf * GT + st8 * 128: hf * GT + (st8 + 1) * 128, :]),
                                      "dq", 16)
                            for q4 in range(4):
                                ipt = q4 % 2
                                wt(te, "vmi", moe_done)
                                wt(te, "vmi", fin["tr_rd"][ipt])
                                for ccc in range(4):
                                    c = q4 * 4 + ccc
                                    ii = te.transpose(out=ps_f[ipt][:, ccc * 128:(ccc + 1) * 128],
                                                      in_=acc[:, c * GT + st8 * 128: c * GT + (st8 + 1) * 128],
                                                      identity=c_identf[:])
                                tv = inc(ii, "tmm")
                                wt(ve, "tmm", tv)
                                wt(ve, "dq", dxo)
                                wt(ve, "dout", fin["o_st"][tb])
                                vv = inc(ve.tensor_tensor(out=osb[:, tb * H + q4 * 512: tb * H + (q4 + 1) * 512],
                                                          in0=ps_f[ipt][:],
                                                          in1=xtm[:, tb * H + q4 * 512: tb * H + (q4 + 1) * 512],
                                                          op=ALU.add), "vmi")
                                fin["tr_rd"][ipt] = vv
                            fin["x_rd"][tb] = cnt["vmi"]
                            wt(sc, "vmi", cnt["vmi"])
                            fin["o_st"][tb] = inc(sc.dma_start(out=out[hf * GT + st8 * 128: hf * GT + (st8 + 1) * 128, :],
                                                               in_=osb[:, tb * H:(tb + 1) * H]), "dout", 16)
                    barrier()

            wt(sc, "dout", cnt["dout"])

    return nc


def _host_inputs(inputs):
    x = np.asarray(inputs["L0_attn_x"], dtype=np.float32)

    def f32(a):
        return np.ascontiguousarray(np.asarray(a, dtype=np.float32))

    def bias_pt(b, nchunk):
        return np.ascontiguousarray(np.asarray(b, np.float32).reshape(nchunk, 128).T)

    shared = {
        "wkvd": f32(inputs["L0_attn_W_kvd"]),
        "wku": f32(inputs["L0_attn_W_ku"]),
        "wvu": f32(inputs["L0_attn_W_vu"]),
        "wkr": f32(inputs["L0_attn_W_kr"]),
        "wqd": f32(inputs["L0_attn_W_qd"]),
        "wqu": f32(inputs["L0_attn_W_qu"]),
        "wqr": f32(inputs["L0_attn_W_qr"]),
        "wo": f32(inputs["L0_attn_W_o"]),
        "bkvd": bias_pt(inputs["L0_attn_b_kvd"], LC),
        "bku": bias_pt(inputs["L0_attn_b_ku"], HC),
        "bkr": bias_pt(inputs["L0_attn_b_kr"], 8),
        "bqd": bias_pt(inputs["L0_attn_b_qd"], QCC),
        "bqu": bias_pt(inputs["L0_attn_b_qu"], HC),
        "bqr": bias_pt(inputs["L0_attn_b_qr"], 8),
        "bo": bias_pt(inputs["L0_attn_b_o"], HC),
        "bvu": f32(inputs["L0_attn_b_vu"]).reshape(1, H),
        "gwt": np.ascontiguousarray(f32(inputs["L0_ffn_gate_w"]).T),
        "wg_t": np.ascontiguousarray(f32(inputs["L0_ffn_w_gate_proj"]).transpose(0, 2, 1)),
        "wu_t": np.ascontiguousarray(f32(inputs["L0_ffn_w_up_proj"]).transpose(0, 2, 1)),
        "wd_t": np.ascontiguousarray(f32(inputs["L0_ffn_w_down_proj"]).transpose(0, 2, 1)),
    }

    inv = (10000.0 ** (-np.arange(0, RD, 2, dtype=np.float32) / RD)).astype(np.float32)
    pos = np.arange(S, dtype=np.float32)
    ang = pos[:, None] * inv[None, :]
    ck = np.cos(ang).T.astype(np.float32)
    sk = np.sin(ang).T.astype(np.float32)
    c4k = np.concatenate([ck, ck, ck, ck], axis=0).astype(BFNP)
    s4k = np.concatenate([-sk, sk, -sk, sk], axis=0).astype(BFNP)

    shared.update({
        "c4k": c4k, "s4k": s4k,
        "identf": np.eye(128, dtype=np.float32),
        "swp": np.eye(128, dtype=np.float32)[np.arange(128) ^ 32].T.astype(BFNP),
        "onescol": np.ones((128, 1), BFNP),
        "onesrow": np.ones((1, 128), BFNP),
    })
    swpm = np.zeros((128, 128), np.float32)
    swpm[np.arange(128) ^ 32, np.arange(128)] = 1.0
    shared["swp"] = swpm.astype(BFNP)

    dnegm = np.zeros((4, 128, 512), np.float32)
    for j in range(4):
        kpos = 128 * j + np.arange(128)
        qpos = np.arange(512)
        dnegm[j] = np.where(kpos[:, None] > qpos[None, :], np.float32(NEG), 0.0)
    shared["dneg"] = dnegm.astype(BFNP)

    onehm = np.zeros((32, NE * 128), np.float32)
    for e in range(NE):
        onehm[e, e * 128:(e + 1) * 128] = 1.0
    shared["oneh"] = onehm.astype(BFNP)

    in_maps = []
    metas = []
    for core in range(NCORE):
        b, j = core // 4, core % 4
        slots = _slots_for_core(j)
        own_pos = np.concatenate([np.arange(512 * s, 512 * (s + 1)) for s in slots])
        d = dict(shared)
        d["xseq"] = np.ascontiguousarray(x[b])
        d["xown"] = np.ascontiguousarray(x[b][own_pos])
        d["c4q"] = np.ascontiguousarray(np.concatenate([ck, ck, ck, ck], 0)[:, own_pos]).astype(BFNP)
        sq = np.concatenate([-sk, sk, -sk, sk], 0)
        d["s4q"] = np.ascontiguousarray(sq[:, own_pos]).astype(BFNP)
        flgim = np.zeros((NSLOT, 4, 128, 128), np.float32)
        bflagm = np.zeros((NSLOT, 128, 16), np.float32)
        for sl in range(NSLOT):
            count = slots[sl] + 1
            for ci in range(4):
                kb = SLOT_LEN[sl] - 4 + ci
                if kb == count - 1:
                    flgim[sl, ci] = np.eye(128, dtype=np.float32)
            for kb in range(16):
                if kb >= count:
                    bflagm[sl, :, kb] = NEG
        d["flgi"] = flgim.astype(BFNP)
        d["bflag"] = bflagm
        in_maps.append(d)
        metas.append((b, own_pos))
    return in_maps, metas


NCORE = 8


def kernel(**inputs):
    in_maps, metas = _host_inputs(inputs)
    nc = build(debug=False)
    res = run_bass_kernel_spmd(nc, in_maps, core_ids=list(range(NCORE)))
    outp = np.zeros((B, S, H), dtype=np.float32)
    for core in range(NCORE):
        b, own_pos = metas[core]
        outp[b, own_pos, :] = res.results[core]["out"]
    return outp
